# version 04
# baseline (speedup 1.0000x reference)
import numpy as np
import ml_dtypes
import concourse.bass as bass
import concourse.mybir as mybir
import concourse.bass_utils as _bu
from concourse.bass_utils import run_bass_kernel_spmd
from contextlib import ExitStack

F32 = mybir.dt.float32
BF16 = mybir.dt.bfloat16
U32 = mybir.dt.uint32
AF = mybir.ActivationFunctionType
ALU = mybir.AluOpType
AX = mybir.AxisListType

D = 1024
H = 16
HD = 64
FF = 4096
NEXP = 8
C = 128
GC = 4
EPS_ATTN = 1e-6
EPS_LN = 1e-5

ENGINES = ("sync", "gpsimd", "scalar", "vector", "tensor")
DBG = False


class Sem:
    def __init__(self, nc, name):
        self.h = nc.alloc_semaphore(name)
        self.v = 0

    def inc(self, n=1):
        self.v += n
        return self.v


class Emitter:
    def __init__(self, nc, blk):
        self.nc = nc
        self.blk = blk
        self.q = {e: [] for e in ENGINES}

    def op(self, eng_name, make, inc=None, n=1):
        if inc is None:
            self.q[eng_name].append(make)
            return None
        tgt = inc.inc(n)
        self.q[eng_name].append(lambda eng, mk=make, s=inc.h, nn=n: mk(eng).then_inc(s, nn))
        return tgt

    def wait(self, eng_name, sem, tgt):
        if not tgt:
            return
        self.q[eng_name].append(lambda eng, s=sem.h, t=tgt: eng.wait_ge(s, t))

    def flush(self):
        for name in ENGINES:
            lst = self.q[name]
            if not lst:
                continue
            getattr(self.blk, name)(lambda eng, L=tuple(lst): [f(eng) for f in L])
            self.q[name] = []


def build(nb, nch):
    T = nch * C
    NG = nch // GC
    ntile = nch
    ntg = T // 256

    nc = bass.Bass("TRN2", target_bir_lowering=False)

    x_d = nc.dram_tensor("x", [nb, T, D], F32, kind="ExternalInput").ap()
    wq_d = nc.dram_tensor("wq", [D, D], F32, kind="ExternalInput").ap()
    wk_d = nc.dram_tensor("wk", [D, D], F32, kind="ExternalInput").ap()
    wv_d = nc.dram_tensor("wv", [D, D], F32, kind="ExternalInput").ap()
    wo_d = nc.dram_tensor("wo", [D, D], F32, kind="ExternalInput").ap()
    proj_d = nc.dram_tensor("proj", [HD, HD], F32, kind="ExternalInput").ap()
    wr_d = nc.dram_tensor("wr", [D, NEXP], F32, kind="ExternalInput").ap()
    w1_d = nc.dram_tensor("w1", [NEXP, D, FF], F32, kind="ExternalInput").ap()
    b1_d = nc.dram_tensor("b1", [NEXP, FF], F32, kind="ExternalInput").ap()
    w2_d = nc.dram_tensor("w2", [NEXP, FF, D], F32, kind="ExternalInput").ap()
    b2_d = nc.dram_tensor("b2", [NEXP, D], F32, kind="ExternalInput").ap()
    idf_d = nc.dram_tensor("idf", [128, 128], F32, kind="ExternalInput").ap()
    idb_d = nc.dram_tensor("idb", [128, 128], BF16, kind="ExternalInput").ap()
    tri_d = nc.dram_tensor("tri", [128, 128], BF16, kind="ExternalInput").ap()
    onesrow_d = nc.dram_tensor("onesrow", [1, 128], F32, kind="ExternalInput").ap()
    onescol_d = nc.dram_tensor("onescol", [128, 1], F32, kind="ExternalInput").ap()
    onesrowb_d = nc.dram_tensor("onesrowb", [1, 128], BF16, kind="ExternalInput").ap()
    iota_d = nc.dram_tensor("iota", [128, 1], F32, kind="ExternalInput").ap()
    iota8_d = nc.dram_tensor("iota8", [1, NEXP], F32, kind="ExternalInput").ap()

    out_d = nc.dram_tensor("out", [nb, T, D], F32, kind="ExternalOutput").ap()

    KI = "ExternalOutput" if DBG else "Internal"
    x1fm_sc = nc.dram_tensor("x1fm_sc", [8, 128, T], BF16, kind=KI).ap()
    x1tm_sc = nc.dram_tensor("x1tm_sc", [ntile, 128, D], F32, kind=KI).ap()
    y0_sc = nc.dram_tensor("y0_sc", [ntile, 128, D], BF16, kind=KI).ap()
    z_sc = nc.dram_tensor("z_sc", [ntile, 128, D], F32, kind=KI).ap()
    b2b_sc = nc.dram_tensor("b2b_sc", [2, D], F32, kind=KI).ap()
    if DBG:
        dbg_d = nc.dram_tensor("dbg", [128, 4096], F32, kind="ExternalOutput").ap()

    ctx = ExitStack()
    sb = lambda n, s, d: ctx.enter_context(nc.sbuf_tensor(n, s, d))
    psb = lambda n, s, d: ctx.enter_context(nc.psum_tensor(n, s, d))

    pb = [psb(f"pb{i}", [128, 512], F32) for i in range(7)]
    pb7b = psb("pb7b", [128, 1024], BF16)

    idf_s = sb("idf_s", [128, 128], F32)
    idb_s = sb("idb_s", [128, 128], BF16)
    tri_s = sb("tri_s", [128, 128], BF16)
    onesrow_s = sb("onesrow_s", [1, 128], F32)
    onescol_s = sb("onescol_s", [128, 1], F32)
    onesrowb_s = sb("onesrowb_s", [1, 128], BF16)
    iota_s = sb("iota_s", [128, 1], F32)
    iota8_s = sb("iota8_s", [1, NEXP], F32)
    epsln_s = sb("epsln_s", [128, 1], F32)
    epsat_s = sb("epsat_s", [128, 1], F32)
    wr_s = sb("wr_s", [128, 8 * NEXP], F32)
    statsM = sb("statsM", [128, 32], F32)
    statsV = sb("statsV", [128, 32], F32)
    rstdA = sb("rstdA", [128, 32], F32)
    rstdB = sb("rstdB", [128, 32], F32)
    rt = sb("rt", [1, 64], F32)
    rtb = sb("rtb", [128, 8], F32)
    idxf = sb("idxf", [128, 80], F32)
    idxu = sb("idxu", [128, 80], U32)
    idxf2 = sb("idxf2", [128, 4], F32)
    idxu2 = sb("idxu2", [128, 4], U32)
    b1s_s = [sb(f"b1s_{s}", [128, 32], F32) for s in range(2)]
    b2rb = [sb(f"b2rb_{s}", [1, D], BF16) for s in range(2)]
    b2g_s = [sb(f"b2g_{s}", [8, 128], F32) for s in range(2)]
    pooled_s = sb("pooled_s", [128, 8], F32)
    logit_s = sb("logit_s", [1, NEXP], F32)
    KVs_e = sb("KVs_e", [128, 8 * 65], BF16)
    KVs_o = sb("KVs_o", [128, 8 * 65], BF16)
    lnst = sb("lnst", [128, 8], F32)
    sqjunk = sb("sqjunk", [128, 1024], BF16)

    sems = {n: Sem(nc, n) for n in
            ("s_ld", "s_st", "s_wl", "s_pe", "s_dve", "s_act", "s_gp", "s_misc")}
    s_ld, s_st, s_wl = sems["s_ld"], sems["s_st"], sems["s_wl"]
    s_pe, s_dve, s_act = sems["s_pe"], sems["s_dve"], sems["s_act"]
    s_gp, s_misc = sems["s_gp"], sems["s_misc"]

    def gp_mark(queue="gpsimd"):
        t = s_gp.inc(1)
        E.q[queue].append(lambda eng, v=t: eng.sem_inc(s_gp.h, 1))
        return t

    blk = ctx.enter_context(nc.Block())
    E = Emitter(nc, blk)

    from concourse.hw_specs import get_activation_tables
    _tabs = list(get_activation_tables(nc.m.arch).values())
    SET_LNEXP = next(i for i, fs in enumerate(_tabs) if AF.Ln in fs and AF.Exp in fs)
    SET_GELU = next(i for i, fs in enumerate(_tabs) if AF.Gelu in fs)

    def act_load(idx):
        E.op("scalar", lambda eng, idx=idx: eng.add_instruction(
            mybir.InstLoadActFuncSet(
                name=nc.get_next_instruction_name(),
                act_func_set_id=idx, ins=[], outs=[])))

    for t, d in [(idf_s, idf_d), (idb_s, idb_d), (tri_s, tri_d),
                 (onesrow_s, onesrow_d), (onescol_s, onescol_d),
                 (onesrowb_s, onesrowb_d), (iota_s, iota_d), (iota8_s, iota8_d)]:
        E.op("sync", lambda eng, t=t, d=d: eng.dma_start(t[:], d), inc=s_misc, n=16)
    for dc in range(8):
        E.op("sync", lambda eng, dc=dc: eng.dma_start(
            wr_s[:, dc * 8:(dc + 1) * 8], wr_d[dc * 128:(dc + 1) * 128, :]),
            inc=s_ld, n=16)
    t_consts = s_misc.v
    t_wr = s_ld.v
    for e in ("vector", "scalar", "tensor", "gpsimd"):
        E.wait(e, s_misc, t_consts)
    E.op("vector", lambda eng: eng.memset(epsln_s[:], EPS_LN))
    E.op("vector", lambda eng: eng.memset(epsat_s[:], EPS_ATTN), inc=s_dve)
    act_load(SET_LNEXP)

    for b in range(nb):
        actx = ExitStack()
        asb = lambda n, s, d: actx.enter_context(nc.sbuf_tensor(f"{n}_b{b}", s, d))
        wqp = asb("wqp", [128, 8 * D], BF16)
        wkp = asb("wkp", [128, 8 * D], BF16)
        wvp = asb("wvp", [128, 8 * D], BF16)
        wop = asb("wop", [128, 8 * D], BF16)
        wstg = asb("wstg", [128, D], F32)
        wstg2 = asb("wstg2", [128, D], BF16)
        wtt = asb("wtt", [128, 128], BF16)
        b2blk = asb("b2blk", [128, 128], BF16)
        projs = asb("projs", [64, 64], F32)
        XR = 4
        xtm = [asb(f"xtm{i}", [128, D], F32) for i in range(XR)]
        xbf = [asb(f"xbf{i}", [128, D], BF16) for i in range(2)]
        xfm = [asb(f"xfm{i}", [128, 8 * 512], BF16) for i in range(2)]
        qf = [asb(f"qf{i}", [128, 8 * 512], BF16) for i in range(2)]
        kf = [asb(f"kf{i}", [128, 8 * 512], BF16) for i in range(2)]
        vaug = [asb(f"vaug{i}", [128, 4 * H * 65], BF16) for i in range(2)]
        kftm_e = asb("kftm_e", [128, D], BF16)
        kftm_o = asb("kftm_o", [128, D], BF16)
        eluT = asb("eluT", [128, 512], F32)
        eluR = asb("eluR", [128, 512], F32)
        eluE = asb("eluE", [128, 512], F32)
        sm = [asb(f"sm{i}", [128, 128], BF16) for i in range(2)]
        dent = asb("dent", [128, 16], F32)
        attn_cat = [asb("attncat0", [128, D], BF16)] * 2
        attn_fm = [asb("attnfm0", [128, D], BF16)] * 2
        res_s = [asb("res0", [128, D], F32)] * 2
        x1tm = [asb(f"x1tm{i}", [128, D], F32) for i in range(2)]
        x1bf = [asb(f"x1bf{i}", [128, D], BF16) for i in range(2)]
        x1fmg = [asb("x1fmg0", [128, 8 * 512], BF16)] * 2

        E.op("sync", lambda eng: eng.dma_start(projs[:], proj_d), inc=s_ld, n=16)
        tpl = s_ld.v
        E.wait("vector", s_ld, tpl)
        E.op("vector", lambda eng: eng.memset(b2blk[:], 0.0))
        E.op("vector", lambda eng: eng.tensor_copy(b2blk[0:64, 0:64], projs[:]))
        E.op("vector", lambda eng: eng.tensor_copy(b2blk[64:128, 64:128], projs[:]),
             inc=s_dve)
        t_b2 = s_dve.v

        prev_cast = 0
        prev_tr_last = 0
        prev_evac = 0
        prev_mm = 0
        prev_wev = 0
        for (wsrc, wdst, premul) in ((wq_d, wqp, True), (wk_d, wkp, True),
                                     (wv_d, wvp, False), (wo_d, wop, False)):
            for dc in range(8):
                E.wait("sync", s_act, prev_cast)
                E.op("sync", lambda eng, wsrc=wsrc, dc=dc: eng.dma_start(
                    wstg[:], wsrc[dc * 128:(dc + 1) * 128, :]), inc=s_ld, n=16)
                tld = s_ld.v
                E.wait("scalar", s_ld, tld)
                if not premul:
                    E.op("scalar", lambda eng, wdst=wdst, dc=dc: eng.copy(
                        wdst[:, dc * D:(dc + 1) * D], wstg[:]), inc=s_act)
                    prev_cast = s_act.v
                    continue
                E.wait("scalar", s_pe, prev_tr_last)
                E.op("scalar", lambda eng: eng.copy(wstg2[:], wstg[:]), inc=s_act)
                prev_cast = s_act.v
                for qc in range(8):
                    E.wait("tensor", s_act, prev_cast)
                    E.wait("tensor", s_dve, prev_evac)
                    E.op("tensor", lambda eng, qc=qc: eng.transpose(
                        pb7b[:, 0:128], wstg2[:, qc * 128:(qc + 1) * 128], idb_s[:]),
                        inc=s_pe)
                    ttr = s_pe.v
                    if qc == 7:
                        prev_tr_last = ttr
                    E.wait("vector", s_pe, ttr)
                    E.wait("vector", s_pe, prev_mm)
                    E.op("vector", lambda eng: eng.tensor_copy(wtt[:], pb7b[:, 0:128]),
                         inc=s_dve)
                    prev_evac = s_dve.v
                    E.wait("tensor", s_dve, prev_evac)
                    E.wait("tensor", s_dve, t_b2)
                    E.wait("tensor", s_act, prev_wev)
                    E.op("tensor", lambda eng: eng.matmul(
                        pb[6][:, 0:128], wtt[:], b2blk[:], start=True, stop=True),
                        inc=s_pe)
                    prev_mm = s_pe.v
                    E.wait("scalar", s_pe, prev_mm)
                    E.op("scalar", lambda eng, wdst=wdst, dc=dc, qc=qc: eng.copy(
                        wdst[:, dc * D + qc * 128: dc * D + (qc + 1) * 128],
                        pb[6][:, 0:128]), inc=s_act)
                    prev_wev = s_act.v

        kv_started = [False, False]
        pool_scale = 1.0 / T

        qkv_ready = {}
        x_loaded = {}
        x_cast = {}
        xtrans_done = {}
        ln_done = {}
        res_done = {}
        div_done = {}
        smask_last = {}
        kvcopy_done = {}
        kvupd_done = {}
        pool_done = {}
        pool_evac = {}
        x1fm_evac = {}
        x1sp = {}
        group_consumed = {}
        bank7_evac = [0]

        def load_x(ch):
            buf = xtm[ch % XR]
            if ch >= XR:
                E.wait("sync", s_dve, ln_done[ch - XR])
            E.op("sync", lambda eng, buf=buf, ch=ch, b=b: eng.dma_start(
                buf[:], x_d[b, ch * C:(ch + 1) * C, :]), inc=s_ld, n=16)
            x_loaded[ch] = s_ld.v

        def cast_x(ch):
            buf = xbf[ch % 2]
            E.wait("scalar", s_ld, x_loaded[ch])
            if ch >= 2:
                E.wait("scalar", s_pe, xtrans_done[ch - 2])
            E.op("scalar", lambda eng, buf=buf, ch=ch: eng.copy(buf[:], xtm[ch % XR][:]),
                 inc=s_act)
            x_cast[ch] = s_act.v

        def trans_x(ch):
            g = ch // GC
            buf = xbf[ch % 2]
            E.wait("tensor", s_act, x_cast[ch])
            E.wait("tensor", s_dve, bank7_evac[0])
            for dc in range(8):
                E.op("tensor", lambda eng, dc=dc, buf=buf: eng.transpose(
                    pb7b[:, dc * 128:(dc + 1) * 128], buf[:, dc * 128:(dc + 1) * 128],
                    idb_s[:]), inc=(s_pe if dc == 7 else None))
            xtrans_done[ch] = s_pe.v
            E.wait("vector", s_pe, xtrans_done[ch])
            dst = xfm[g % 2]
            tc_ = ch % GC
            E.op("vector", lambda eng, dst=dst, tc_=tc_: eng.tensor_copy(
                dst[:].rearrange("p (dc t) -> p dc t", dc=8)[:, :, tc_ * 128:(tc_ + 1) * 128],
                pb7b[:].rearrange("p (dc t) -> p dc t", dc=8)), inc=s_dve)
            bank7_evac[0] = s_dve.v

        def proj_group(g):
            xg = xfm[g % 2]
            qd, kd, vd = qf[g % 2], kf[g % 2], vaug[g % 2]
            E.wait("tensor", s_dve, s_dve.v)
            if g >= 2:
                E.wait("vector", s_pe, group_consumed[g - 2])
            E.op("vector", lambda eng, vd=vd: eng.memset(vd[:], 1.0), inc=s_dve)
            prev_add = 0
            elu_min = {}
            for wi, (wsrc, dst, bks) in enumerate(((wqp, qd, (2, 3)), (wkp, kd, (4, 5)))):
                for pc in range(8):
                    bk = bks[pc % 2]
                    if pc >= 2:
                        E.wait("tensor", s_dve, elu_min[(wi, pc - 2)])
                    for dc in range(8):
                        E.op("tensor", lambda eng, wsrc=wsrc, dc=dc, pc=pc, bk=bk, xg=xg:
                             eng.matmul(pb[bk][:],
                                        wsrc[:, dc * D + pc * 128:dc * D + (pc + 1) * 128],
                                        xg[:, dc * 512:(dc + 1) * 512],
                                        start=(dc == 0), stop=(dc == 7)),
                             inc=(s_pe if dc == 7 else None))
                    tmm = s_pe.v
                    E.wait("scalar", s_pe, tmm)
                    E.wait("scalar", s_dve, prev_add)
                    E.op("scalar", lambda eng, bk=bk: eng.activation(
                        eluR[:], pb[bk][:], AF.Relu), inc=s_act)
                    trelu = s_act.v
                    E.wait("vector", s_act, trelu)
                    E.op("vector", lambda eng, bk=bk: eng.tensor_scalar(
                        eluT[:], pb[bk][:], 0.0, None, ALU.min), inc=s_dve)
                    elu_min[(wi, pc)] = s_dve.v
                    E.wait("scalar", s_dve, elu_min[(wi, pc)])
                    E.op("scalar", lambda eng: eng.activation(
                        eluE[:], eluT[:], AF.Exp), inc=s_act)
                    texp = s_act.v
                    E.wait("vector", s_act, texp)
                    E.op("vector", lambda eng, dst=dst, pc=pc: eng.tensor_add(
                        dst[:, pc * 512:(pc + 1) * 512], eluE[:], eluR[:]), inc=s_dve)
                    prev_add = s_dve.v
            prev_vev = 0
            for tc_ in range(4):
                bka, bkb = (2, 3) if tc_ % 2 == 0 else (4, 5)
                if tc_ < 2:
                    E.wait("tensor", s_dve, elu_min[(1, 6 + tc_)])
                else:
                    E.wait("tensor", s_dve, prev_vev)
                for dc in range(8):
                    E.op("tensor", lambda eng, dc=dc, tc_=tc_, bka=bka, xg=xg:
                         eng.matmul(pb[bka][:],
                                    xg[:, dc * 512 + tc_ * 128:dc * 512 + (tc_ + 1) * 128],
                                    wvp[:, dc * D:dc * D + 512],
                                    start=(dc == 0), stop=(dc == 7)), inc=None)
                    E.op("tensor", lambda eng, dc=dc, tc_=tc_, bkb=bkb, xg=xg:
                         eng.matmul(pb[bkb][:],
                                    xg[:, dc * 512 + tc_ * 128:dc * 512 + (tc_ + 1) * 128],
                                    wvp[:, dc * D + 512:(dc + 1) * D],
                                    start=(dc == 0), stop=(dc == 7)),
                         inc=(s_pe if dc == 7 else None))
                tv = s_pe.v
                E.wait("vector", s_pe, tv)
                E.op("vector", lambda eng, vd=vd, tc_=tc_, bka=bka: eng.tensor_copy(
                    vd[:].rearrange("p (t h e) -> p t h e", t=4, h=H)[:, tc_, 0:8, 0:64],
                    pb[bka][:].rearrange("p (h e) -> p h e", h=8)), inc=None)
                E.op("vector", lambda eng, vd=vd, tc_=tc_, bkb=bkb: eng.tensor_copy(
                    vd[:].rearrange("p (t h e) -> p t h e", t=4, h=H)[:, tc_, 8:16, 0:64],
                    pb[bkb][:].rearrange("p (h e) -> p h e", h=8)), inc=s_dve)
                prev_vev = s_dve.v
            qkv_ready[g] = s_dve.v

        def outaug_region(h):
            if h < 7:
                return 4, (h % 7) * 65
            if h < 14:
                return 5, (h - 7) * 65
            return 6, (h - 14) * 65

        def scan_chunk(ch):
            g = ch // GC
            tc_ = ch % GC
            qd, kd, vd = qf[g % 2], kf[g % 2], vaug[g % 2]
            par = ch % 2
            E.wait("tensor", s_dve, qkv_ready[g])
            E.wait("tensor", s_dve, bank7_evac[0])
            for dc in range(8):
                E.op("tensor", lambda eng, dc=dc, kd=kd, tc_=tc_: eng.transpose(
                    pb7b[:, dc * 128:(dc + 1) * 128],
                    kd[:, dc * 512 + tc_ * 128:dc * 512 + (tc_ + 1) * 128], idb_s[:]),
                    inc=(s_pe if dc == 7 else None))
            tkt = s_pe.v
            E.wait("vector", s_pe, tkt)
            if ch == 0:
                E.op("vector", lambda eng: eng.memset(pooled_s[:], 0.0))
                E.op("vector", lambda eng: eng.memset(kftm_e[:], 0.0))
                E.op("vector", lambda eng: eng.memset(kftm_o[:], 0.0))
                E.op("vector", lambda eng: eng.memset(KVs_e[:], 0.0))
                E.op("vector", lambda eng: eng.memset(KVs_o[:], 0.0))
            E.op("vector", lambda eng: eng.tensor_copy(
                kftm_e[:].rearrange("p (hc d) -> p hc d", hc=8)[:, :, 0:64],
                pb7b[:].rearrange("p (hc d) -> p hc d", hc=8)[:, :, 0:64]), inc=None)
            E.op("vector", lambda eng: eng.tensor_copy(
                kftm_o[:].rearrange("p (hc d) -> p hc d", hc=8)[:, :, 64:128],
                pb7b[:].rearrange("p (hc d) -> p hc d", hc=8)[:, :, 64:128]), inc=s_dve)
            bank7_evac[0] = s_dve.v
            tkf_evac = s_dve.v

            E.wait("vector", s_pe, kvupd_done.get(ch - 1, 0))
            if ch == 0:
                E.op("vector", lambda eng: eng.memset(lnst[:, 7:8], 0.0), inc=s_dve)
            else:
                E.op("vector", lambda eng: eng.tensor_copy(
                    KVs_e[0:64, 0:455], pb[0][0:64, 0:455]), inc=None)
                E.op("vector", lambda eng: eng.tensor_copy(
                    KVs_e[0:64, 455:520], pb[1][0:64, 0:65]), inc=None)
                E.op("vector", lambda eng: eng.tensor_copy(
                    KVs_o[64:128, 0:455], pb[0][64:128, 0:455]), inc=None)
                E.op("vector", lambda eng: eng.tensor_copy(
                    KVs_o[64:128, 455:520], pb[1][64:128, 0:65]), inc=s_dve)
            kvcopy_done[ch] = s_dve.v

            E.wait("tensor", s_dve, kvcopy_done[ch])
            E.wait("tensor", s_dve, res_done.get(ch - 1, 0))
            E.wait("tensor", s_dve, pool_evac.get(ch - 1, 0))
            smask = {}
            outp = {}

            def head_slices(h):
                pq = (h % 2) * 64
                hc = h // 2
                qsl = qd[pq:pq + 64, hc * 512 + tc_ * 128: hc * 512 + (tc_ + 1) * 128]
                ksl = kd[pq:pq + 64, hc * 512 + tc_ * 128: hc * 512 + (tc_ + 1) * 128]
                vsl = vd[:, tc_ * H * 65 + h * 65: tc_ * H * 65 + (h + 1) * 65]
                qpair = qd[:, hc * 512 + tc_ * 128: hc * 512 + (tc_ + 1) * 128]
                kvz = (KVs_e if h % 2 == 0 else KVs_o)[:, hc * 65:(hc + 1) * 65]
                return qsl, ksl, vsl, qpair, kvz

            def emit_S(h):
                qsl, ksl, _, _, _ = head_slices(h)
                sbk = 2 + (h % 2)
                if h >= 2:
                    E.wait("tensor", s_dve, smask[h - 2])
                E.op("tensor", lambda eng, sbk=sbk, ksl=ksl, qsl=qsl: eng.matmul(
                    pb[sbk][:, 0:128], ksl, qsl, start=True, stop=True), inc=s_pe)
                tS = s_pe.v
                E.wait("vector", s_pe, tS)
                if h >= 2:
                    E.wait("vector", s_pe, outp[h - 2])
                E.op("vector", lambda eng, sbk=sbk, h=h: eng.tensor_mul(
                    sm[h % 2][:], pb[sbk][:, 0:128], tri_s[:]), inc=s_dve)
                smask[h] = s_dve.v

            def emit_outaug(h):
                _, _, vsl, qpair, kvz = head_slices(h)
                obk, off = outaug_region(h)
                E.wait("tensor", s_dve, smask[h])
                fib = h in (0, 7, 14)
                E.op("tensor", lambda eng, obk=obk, off=off, h=h, vsl=vsl, fib=fib:
                     eng.matmul(pb[obk][:, off:off + 65], sm[h % 2][:], vsl,
                                start=fib, stop=False, skip_group_check=True), inc=None)
                E.op("tensor", lambda eng, obk=obk, off=off, qpair=qpair, kvz=kvz:
                     eng.matmul(pb[obk][:, off:off + 65], qpair, kvz,
                                start=False, stop=True, skip_group_check=True), inc=s_pe)
                outp[h] = s_pe.v

            emit_S(0)
            for h in range(H):
                if h + 1 < H:
                    emit_S(h + 1)
                emit_outaug(h)
            smask_last[ch] = smask[H - 1]

            E.wait("tensor", s_dve, tkf_evac)
            for h in range(H):
                hc = h // 2
                kvb = 0 if hc < 7 else 1
                off = hc * 65 if hc < 7 else 0
                vsl = vd[:, tc_ * H * 65 + h * 65: tc_ * H * 65 + (h + 1) * 65]
                ksl_tm = (kftm_e if h % 2 == 0 else kftm_o)[:, hc * 128:(hc + 1) * 128]
                st = (ch == 0) and (not kv_started[kvb])
                if st:
                    kv_started[kvb] = True
                E.op("tensor", lambda eng, kvb=kvb, off=off, ksl_tm=ksl_tm,
                     vsl=vsl, st=st, ch=ch:
                     eng.matmul(pb[kvb][:, off:off + 65], ksl_tm, vsl,
                                start=st, stop=(ch == nch - 1), skip_group_check=True),
                     inc=(s_pe if h == H - 1 else None))
            kvupd_done[ch] = s_pe.v

            cat = attn_cat[par]
            E.wait("vector", s_pe, outp[H - 1])
            E.op("vector", lambda eng: eng.tensor_scalar(
                dent[:, 0:7].rearrange("p (h e) -> p h e", e=1),
                pb[4][:, 0:455].rearrange("p (h e) -> p h e", e=65)[:, :, 64:65],
                epsat_s[:, 0:1], None, ALU.add), inc=None)
            E.op("vector", lambda eng: eng.tensor_scalar(
                dent[:, 7:14].rearrange("p (h e) -> p h e", e=1),
                pb[5][:, 0:455].rearrange("p (h e) -> p h e", e=65)[:, :, 64:65],
                epsat_s[:, 0:1], None, ALU.add), inc=None)
            E.op("vector", lambda eng: eng.tensor_scalar(
                dent[:, 14:16].rearrange("p (h e) -> p h e", e=1),
                pb[6][:, 0:130].rearrange("p (h e) -> p h e", e=65)[:, :, 64:65],
                epsat_s[:, 0:1], None, ALU.add), inc=s_dve)
            tden = s_dve.v
            E.wait("vector", s_dve, tden)
            E.op("vector", lambda eng: eng.reciprocal(dent[:, 0:16], dent[:, 0:16]),
                 inc=s_dve)
            trec = s_dve.v
            E.wait("vector", s_dve, trec)
            for h in range(H):
                obk, off = outaug_region(h)
                E.op("vector", lambda eng, obk=obk, off=off, h=h, cat=cat:
                     eng.tensor_scalar(cat[:, h * 64:(h + 1) * 64],
                                       pb[obk][:, off:off + 64],
                                       dent[:, h:h + 1], None, ALU.mult),
                     inc=(s_dve if h == H - 1 else None))
            div_done[ch] = s_dve.v

            if DBG and ch == -1 and b == 0:
                E.wait("vector", s_dve, div_done[ch])
                E.op("vector", lambda eng: eng.tensor_copy(sqjunk[:], cat[:]), inc=s_dve)
                tdbg = s_dve.v
                E.wait("gpsimd", s_dve, tdbg)
                E.op("gpsimd", lambda eng: eng.dma_start(dbg_d[:, 0:1024], sqjunk[:]),
                     inc=s_st, n=16)
                E.op("gpsimd", lambda eng: eng.dma_start(dbg_d[:, 1024:1040], dent[:]),
                     inc=s_st, n=16)
                E.op("gpsimd", lambda eng, vd=vd: eng.dma_start(
                    dbg_d[:, 1040:1105].rearrange("p e -> p e"),
                    vd[:, 0:65]), inc=s_st, n=16)
                E.op("gpsimd", lambda eng: eng.dma_start(
                    dbg_d[:, 1105:1625], KVs_e[:]), inc=s_st, n=16)

            E.wait("tensor", s_dve, div_done[ch])
            E.wait("tensor", s_dve, bank7_evac[0])
            for dc in range(8):
                E.op("tensor", lambda eng, dc=dc, cat=cat: eng.transpose(
                    pb7b[:, dc * 128:(dc + 1) * 128], cat[:, dc * 128:(dc + 1) * 128],
                    idb_s[:]), inc=(s_pe if dc == 7 else None))
            tat = s_pe.v
            E.wait("vector", s_pe, tat)
            E.op("vector", lambda eng, par=par: eng.tensor_copy(attn_fm[par][:], pb7b[:]),
                 inc=s_dve)
            bank7_evac[0] = s_dve.v
            tafm = s_dve.v

            E.wait("tensor", s_dve, tafm)
            for half, bk in ((0, 4), (1, 5)):
                for hc2 in range(8):
                    E.op("tensor", lambda eng, hc2=hc2, half=half, bk=bk, par=par:
                         eng.matmul(pb[bk][:],
                                    attn_fm[par][:, hc2 * 128:(hc2 + 1) * 128],
                                    wop[:, hc2 * D + half * 512: hc2 * D + (half + 1) * 512],
                                    start=(hc2 == 0), stop=(hc2 == 7)),
                         inc=(s_pe if (hc2 == 7 and half == 1) else None))
            two = s_pe.v

            xres = xtm[ch % XR]
            rr = res_s[par]
            E.wait("vector", s_pe, two)
            E.op("vector", lambda eng, rr=rr, xres=xres: eng.tensor_add(
                rr[:, 0:512], pb[4][:], xres[:, 0:512]), inc=None)
            E.op("vector", lambda eng, rr=rr, xres=xres: eng.tensor_add(
                rr[:, 512:1024], pb[5][:], xres[:, 512:1024]), inc=s_dve)
            tres = s_dve.v
            res_done[ch] = tres
            E.wait("vector", s_dve, tres)
            E.op("vector", lambda eng, rr=rr: eng.reduce_sum(
                lnst[:, 0:1], rr[:], axis=AX.X), inc=s_dve)
            tsum = s_dve.v
            E.wait("vector", s_dve, tsum)
            E.op("vector", lambda eng: eng.tensor_scalar(
                lnst[:, 1:2], lnst[:, 0:1], 1.0 / D, None, ALU.mult), inc=s_dve)
            tmean = s_dve.v
            E.wait("scalar", s_dve, tres)
            E.op("scalar", lambda eng, rr=rr: eng.activation(
                sqjunk[:], rr[:], AF.Square, accum_out=lnst[:, 2:3]), inc=s_act)
            tsq = s_act.v
            E.wait("vector", s_act, tsq)
            E.op("vector", lambda eng: eng.tensor_scalar(
                lnst[:, 3:4], lnst[:, 2:3], 1.0 / D, None, ALU.mult), inc=s_dve)
            tmsq = s_dve.v
            E.wait("vector", s_dve, tmsq)
            E.op("vector", lambda eng: eng.scalar_tensor_tensor(
                lnst[:, 4:5], lnst[:, 1:2], lnst[:, 1:2], lnst[:, 3:4],
                ALU.mult, ALU.subtract), inc=s_dve)
            tnv = s_dve.v
            E.wait("scalar", s_dve, tnv)
            E.op("scalar", lambda eng: eng.activation(
                lnst[:, 5:6], lnst[:, 4:5], AF.Ln, bias=epsln_s[:, 0:1], scale=-1.0),
                inc=s_act)
            tln = s_act.v
            E.wait("scalar", s_act, tln)
            E.op("scalar", lambda eng: eng.activation(
                lnst[:, 6:7], lnst[:, 5:6], AF.Exp, scale=-0.5), inc=s_act)
            trstd = s_act.v
            E.wait("vector", s_act, trstd)
            if ch >= 2:
                E.wait("vector", s_st, x1sp[ch - 2])
            x1t = x1tm[par]
            E.op("vector", lambda eng, rr=rr, x1t=x1t: eng.tensor_scalar(
                x1t[:], rr[:], lnst[:, 1:2], lnst[:, 6:7], ALU.subtract, ALU.mult),
                inc=s_dve)
            ln_done[ch] = s_dve.v

            if DBG and ch == 0 and b == 0:
                E.wait("gpsimd", s_dve, ln_done[ch])
                E.op("gpsimd", lambda eng, rr=rr: eng.dma_start(
                    dbg_d[:, 2048:3072], rr[:]), inc=s_st, n=16)
                E.op("gpsimd", lambda eng: eng.dma_start(
                    dbg_d[:, 3072:3080], lnst[:]), inc=s_st, n=16)

            E.wait("tensor", s_dve, ln_done[ch])
            for dc in range(8):
                E.op("tensor", lambda eng, dc=dc, x1t=x1t: eng.matmul(
                    pb[6][:, 132 + dc:133 + dc], x1t[:, dc * 128:(dc + 1) * 128],
                    onescol_s[:], start=(dc == 0), stop=(dc == 7),
                    skip_group_check=True),
                    inc=(s_pe if dc == 7 else None))
            pool_done[ch] = s_pe.v
            E.wait("vector", s_pe, pool_done[ch])
            E.op("vector", lambda eng: eng.tensor_add(
                pooled_s[:], pooled_s[:], pb[6][:, 132:140]), inc=s_dve)
            pool_evac[ch] = s_dve.v

            E.wait("gpsimd", s_dve, ln_done[ch])
            E.op("gpsimd", lambda eng, x1t=x1t, ch=ch: eng.dma_start(
                x1tm_sc[ch], x1t[:]), inc=s_st, n=16)
            x1sp[ch] = s_st.v
            E.wait("scalar", s_dve, ln_done[ch])
            xb1 = x1bf[par]
            E.op("scalar", lambda eng, xb1=xb1, x1t=x1t: eng.copy(xb1[:], x1t[:]),
                 inc=s_act)
            tx1c = s_act.v
            E.wait("tensor", s_act, tx1c)
            E.wait("tensor", s_dve, bank7_evac[0])
            for dc in range(8):
                E.op("tensor", lambda eng, dc=dc, xb1=xb1: eng.transpose(
                    pb7b[:, dc * 128:(dc + 1) * 128], xb1[:, dc * 128:(dc + 1) * 128],
                    idb_s[:]), inc=(s_pe if dc == 7 else None))
            tx1t = s_pe.v
            E.wait("vector", s_pe, tx1t)
            if tc_ == 0 and g >= 1:
                E.wait("vector", s_st, x1sp[("fm", g - 1)])
            gdst = x1fmg[g % 2]
            E.op("vector", lambda eng, gdst=gdst, tc_=tc_: eng.tensor_copy(
                gdst[:].rearrange("p (dc t) -> p dc t", dc=8)[:, :, tc_ * 128:(tc_ + 1) * 128],
                pb7b[:].rearrange("p (dc t) -> p dc t", dc=8)), inc=s_dve)
            bank7_evac[0] = s_dve.v
            x1fm_evac[ch] = s_dve.v
            if tc_ == GC - 1:
                E.wait("gpsimd", s_dve, x1fm_evac[ch])
                E.op("gpsimd", lambda eng, gdst=gdst, g=g: eng.dma_start(
                    x1fm_sc[:, :, g * 512:(g + 1) * 512].rearrange("dc p t -> p dc t"),
                    gdst[:].rearrange("p (dc t) -> p dc t", dc=8)), inc=s_st, n=16)
                x1sp[("fm", g)] = s_st.v
                group_consumed[g] = s_pe.v

        for ch in range(min(XR, nch)):
            load_x(ch)
        for ch in range(min(GC, nch)):
            cast_x(ch)
            trans_x(ch)
        for g in range(NG):
            proj_group(g)
            for ch in range(g * GC, (g + 1) * GC):
                scan_chunk(ch)
            for ch in range((g + 1) * GC, min((g + 2) * GC, nch)):
                if ch >= XR:
                    load_x(ch)
                cast_x(ch)
                trans_x(ch)

        E.wait("vector", s_dve, pool_evac[nch - 1])
        E.op("vector", lambda eng: eng.tensor_scalar(
            pooled_s[:], pooled_s[:], pool_scale, None, ALU.mult), inc=s_dve)
        tpooled = s_dve.v
        E.wait("tensor", s_dve, tpooled)
        for dc in range(8):
            E.op("tensor", lambda eng, dc=dc: eng.matmul(
                pb[6][0:1, 0:8], pooled_s[:, dc:dc + 1], wr_s[:, dc * 8:(dc + 1) * 8],
                start=(dc == 0), stop=(dc == 7)), inc=(s_pe if dc == 7 else None))
        tlog = s_pe.v
        E.wait("vector", s_pe, tlog)

        def vsedge(make):
            E.op("vector", make, inc=s_dve)
            E.wait("vector", s_dve, s_dve.v)

        vsedge(lambda eng: eng.tensor_copy(logit_s[:], pb[6][0:1, 0:8]))
        vsedge(lambda eng: eng.reduce_max(rt[0:1, 0:1], logit_s[:], axis=AX.X))
        vsedge(lambda eng: eng.tensor_scalar(rt[0:1, 16:24], logit_s[:], rt[0:1, 0:1],
                                             None, ALU.subtract))
        vsedge(lambda eng: eng.tensor_scalar(rt[0:1, 24:32], rt[0:1, 16:24], 0.0,
                                             None, ALU.is_ge))
        vsedge(lambda eng: eng.tensor_mul(rt[0:1, 32:40], rt[0:1, 24:32], iota8_s[:]))
        vsedge(lambda eng: eng.reduce_sum(rt[0:1, 2:3], rt[0:1, 32:40], axis=AX.X))
        vsedge(lambda eng: eng.scalar_tensor_tensor(rt[0:1, 40:48], rt[0:1, 24:32],
                                                    1e30, logit_s[:], ALU.mult,
                                                    ALU.subtract))
        vsedge(lambda eng: eng.tensor_reduce(rt[0:1, 1:2], rt[0:1, 40:48], AX.X,
                                             ALU.min, negate=True))
        vsedge(lambda eng: eng.tensor_scalar(rt[0:1, 48:56], logit_s[:], rt[0:1, 1:2],
                                             None, ALU.is_ge))
        vsedge(lambda eng: eng.tensor_sub(rt[0:1, 56:64], rt[0:1, 48:56], rt[0:1, 24:32]))
        vsedge(lambda eng: eng.tensor_mul(rt[0:1, 32:40], rt[0:1, 56:64], iota8_s[:]))
        vsedge(lambda eng: eng.reduce_sum(rt[0:1, 3:4], rt[0:1, 32:40], axis=AX.X))
        vsedge(lambda eng: eng.tensor_scalar(rt[0:1, 4:5], rt[0:1, 0:1], -1.0,
                                             None, ALU.mult))
        tprev = s_dve.v
        E.wait("scalar", s_dve, tprev)
        E.op("scalar", lambda eng: eng.activation(rt[0:1, 5:6], rt[0:1, 1:2], AF.Exp,
                                                  bias=rt[0:1, 4:5]), inc=s_act)
        E.wait("vector", s_act, s_act.v)
        vsedge(lambda eng: eng.tensor_scalar(rt[0:1, 6:7], rt[0:1, 5:6], 1.0,
                                             None, ALU.add))
        vsedge(lambda eng: eng.reciprocal(rt[0:1, 7:8], rt[0:1, 6:7]))
        vsedge(lambda eng: eng.tensor_mul(rt[0:1, 8:9], rt[0:1, 5:6], rt[0:1, 7:8]))
        trt = s_dve.v

        E.wait("tensor", s_dve, trt)
        for i, col in enumerate((2, 3, 7, 8)):
            E.op("tensor", lambda eng, i=i, col=col: eng.matmul(
                pb[6][:, 128 + i:129 + i], onesrow_s[:], rt[0:1, col:col + 1],
                start=True, stop=True, skip_group_check=True),
                inc=(s_pe if i == 3 else None))
        tbc = s_pe.v
        E.wait("vector", s_pe, tbc)
        vsedge(lambda eng: eng.tensor_scalar(rtb[:, 0:2], pb[6][:, 128:130], 1024.0,
                                             None, ALU.mult))
        vsedge(lambda eng: eng.tensor_copy(rtb[:, 2:4], pb[6][:, 130:132]))
        vsedge(lambda eng: eng.tensor_scalar(rtb[:, 4:6], pb[6][:, 128:130], 4096.0,
                                             None, ALU.mult))
        vsedge(lambda eng: eng.tensor_scalar(rtb[:, 6:8], pb[6][:, 128:130], 32.0,
                                             None, ALU.mult))
        t_rtb = s_dve.v
        for s in range(2):
            for dc in range(8):
                E.op("vector", lambda eng, s=s, dc=dc: eng.tensor_scalar(
                    idxf[:, s * 8 + dc: s * 8 + dc + 1], iota_s[:],
                    rtb[:, s:s + 1], float(dc * 128), ALU.add, ALU.add), inc=None)
            for fc in range(32):
                E.op("vector", lambda eng, s=s, fc=fc: eng.tensor_scalar(
                    idxf[:, 16 + s * 32 + fc: 17 + s * 32 + fc], iota_s[:],
                    rtb[:, 4 + s:5 + s], float(fc * 128), ALU.add, ALU.add), inc=None)
            E.op("vector", lambda eng, s=s: eng.tensor_scalar(
                idxf2[:, s:s + 1], iota_s[:], rtb[:, 6 + s:7 + s], None, ALU.add),
                inc=None)
            E.op("vector", lambda eng, s=s: eng.tensor_scalar(
                idxf2[:, 2 + s:3 + s], rtb[:, 6 + s:7 + s], 0.25, None, ALU.mult),
                inc=s_dve)
        tidxf = s_dve.v
        E.wait("vector", s_dve, tidxf)
        E.op("vector", lambda eng: eng.tensor_scalar(
            idxf2[:, 2:4], idxf2[:, 2:4], iota_s[:, 0:1], None, ALU.add), inc=s_dve)
        tidxf2 = s_dve.v
        E.wait("vector", s_dve, tidxf2)
        E.op("vector", lambda eng: eng.tensor_copy(idxu[:], idxf[:]), inc=None)
        E.op("vector", lambda eng: eng.tensor_copy(idxu2[:], idxf2[:]), inc=s_dve)
        tidx = s_dve.v

        E.wait("gpsimd", s_st, s_st.v)
        mark_attn = gp_mark()
        E.wait("gpsimd", s_dve, tidx)
        for s in range(2):
            E.op("gpsimd", lambda eng, s=s: eng.indirect_dma_start(
                b2g_s[s][:], None, b2_d.rearrange("e (r f) -> e r f", f=128),
                bass.IndirectOffsetOnAxis(idxu2[0:8, 2 + s:3 + s], 1)), inc=s_misc, n=16)
            tb2g = s_misc.v
            E.wait("gpsimd", s_misc, tb2g)
            E.op("gpsimd", lambda eng, s=s: eng.dma_start(
                b2b_sc[s].rearrange("(p f) -> p f", f=128), b2g_s[s][:]),
                inc=s_misc, n=16)
        E.wait("gpsimd", s_misc, s_misc.v)
        for s in range(2):
            E.op("gpsimd", lambda eng, s=s: eng.dma_start(
                b2rb[s][:], b2b_sc[s:s + 1, :]), inc=s_misc, n=16)
        E.wait("gpsimd", s_misc, s_misc.v)
        mark_b2 = gp_mark()

        E.flush()
        actx.close()

        fctx = ExitStack()
        fsb = lambda n, s, d: fctx.enter_context(nc.sbuf_tensor(f"{n}_b{b}", s, d))
        b1r = [fsb(f"b1r_{s}", [32, 128], F32) for s in range(2)]
        w1s = fsb("w1s", [128, 8 * FF], BF16)
        w2s = fsb("w2s", [128, 32 * D], BF16)
        x1t_f = [fsb(f"x1t_f{i}", [128, 8 * 512], BF16) for i in range(2)]
        hcache = fsb("hcache", [128, 32 * 512], BF16)
        y0c = [fsb(f"y0c{i}", [128, D], BF16) for i in range(2)]
        x1c = [fsb("x1c0", [128, D], F32)] * 2
        zc = [fsb(f"zc{i}", [128, D], F32) for i in range(2)]
        zin = x1c
        oout = zc

        tb2cast = None

        E.wait("gpsimd", s_dve, tidx)
        for s in range(2):
            E.op("gpsimd", lambda eng, s=s: eng.indirect_dma_start(
                b1r[s][:], None, b1_d.rearrange("e (r f) -> e r f", f=128),
                bass.IndirectOffsetOnAxis(idxu2[0:32, s:s + 1], 1)), inc=s_misc, n=16)
        E.wait("gpsimd", s_misc, s_misc.v)
        mark_b1 = gp_mark()
        E.wait("tensor", s_gp, mark_b1)
        E.wait("tensor", s_dve, t_rtb)
        for s in range(2):
            E.op("tensor", lambda eng, s=s: eng.transpose(
                pb[6][:, 128 + s * 32:160 + s * 32], b1r[s][:], idf_s[0:32, 0:32]), inc=s_pe)
        tb1t = s_pe.v
        E.wait("vector", s_pe, tb1t)
        for s in range(2):
            E.op("vector", lambda eng, s=s: eng.tensor_copy(
                b1s_s[s][:, 0:32], pb[6][:, 128 + s * 32:160 + s * 32]),
                inc=(s_dve if s == 1 else None))
        tb1 = s_dve.v

        t_attn_dve = s_dve.v
        act_load(SET_GELU)
        slot_pe_done = {}
        slot_dve_done = {}
        for slot in range(2):
            wl_w1 = {}
            wl_w2 = {}
            E.wait("gpsimd", s_dve, tidx)
            if slot == 1:
                E.wait("gpsimd", s_pe, slot_pe_done[0])
            for half in range(2):
                for dc in range(8):
                    E.op("gpsimd", lambda eng, dc=dc, half=half, slot=slot:
                         eng.indirect_dma_start(
                             w1s[:, dc * FF + half * 2048: dc * FF + (half + 1) * 2048],
                             None, w1_d,
                             bass.IndirectOffsetOnAxis(
                                 idxu[:, slot * 8 + dc: slot * 8 + dc + 1], 1),
                             element_offset=half * 2048), inc=s_wl, n=16)
                    wl_w1[half * 8 + dc] = s_wl.v
                rng_fc = range(16) if half == 0 else range(16, 32)
                for fc in rng_fc:
                    E.op("gpsimd", lambda eng, fc=fc, slot=slot:
                         eng.indirect_dma_start(
                             w2s[:, fc * D:(fc + 1) * D], None, w2_d,
                             bass.IndirectOffsetOnAxis(
                                 idxu[:, 16 + slot * 32 + fc: 17 + slot * 32 + fc], 1)),
                         inc=s_wl, n=16)
                    wl_w2[fc] = s_wl.v

            E.wait("scalar", s_dve, tb1)
            x1_tile_free = {}
            ybank_free = {}
            zstore = {}
            zdone = {}
            if slot == 1:
                E.wait("sync", s_gp, mark_slot0)
            nsup = ntg // 2 if ntg >= 2 else 1
            for sup in range(nsup):
                buf = x1t_f[sup % 2]
                if sup == 0:
                    E.wait("sync", s_gp, mark_attn)
                if sup >= 2:
                    E.wait("sync", s_pe, x1_tile_free[sup - 2])
                elif slot == 1:
                    E.wait("sync", s_pe, slot_pe_done[0])
                E.op("sync", lambda eng, buf=buf, sup=sup: eng.dma_start(
                    buf[:].rearrange("p (dc t) -> p dc t", dc=8),
                    x1fm_sc[:, :, sup * 512:(sup + 1) * 512]
                    .rearrange("dc p t -> p dc t")), inc=s_ld, n=16)
                tx1l = s_ld.v
                E.wait("tensor", s_ld, tx1l)
                if sup == 0 and slot == 0:
                    E.wait("tensor", s_dve, t_attn_dve)
                if sup == 0 and slot == 1:
                    E.wait("tensor", s_dve, slot_dve_done[0])
                if sup == 0:
                    E.wait("tensor", s_gp, mark_b2)

                hg_t = {}
                for fc in range(32):
                    hbk = fc % 2
                    if sup == 0:
                        E.wait("tensor", s_wl, wl_w1[(fc // 16) * 8 + 7])
                        E.wait("tensor", s_wl, wl_w2[fc])
                    if fc >= 2:
                        E.wait("tensor", s_act, hg_t[fc - 2])
                    for dc in range(8):
                        E.op("tensor", lambda eng, dc=dc, fc=fc, hbk=hbk, buf=buf:
                             eng.matmul(pb[hbk][:],
                                        w1s[:, dc * FF + fc * 128: dc * FF + (fc + 1) * 128],
                                        buf[:, dc * 512:(dc + 1) * 512],
                                        start=(dc == 0), stop=(dc == 7),
                                        skip_group_check=True),
                             inc=(s_pe if dc == 7 else None))
                    th_ = s_pe.v
                    E.wait("scalar", s_pe, th_)
                    E.op("scalar", lambda eng, fc=fc, hbk=hbk, slot=slot: eng.activation(
                        hcache[:, fc * 512:(fc + 1) * 512], pb[hbk][:], AF.Gelu,
                        bias=b1s_s[slot][:, fc:fc + 1]), inc=s_act)
                    hg_t[fc] = s_act.v
                x1_tile_free[sup] = s_pe.v
                E.wait("tensor", s_act, hg_t[31])
                E.wait("tensor", s_act, hg_t[30])

                for tgh in range(2):
                    tg = sup * 2 + tgh
                    toff = tgh * 256
                    if tg >= 1:
                        E.wait("tensor", s_dve, ybank_free[tg - 1])
                    for th, bks in ((0, (2, 3)), (1, (4, 5))):
                        for halfd in range(2):
                            E.op("tensor", lambda eng, bk=bks[halfd], halfd=halfd, slot=slot:
                                 eng.matmul(pb[bk][:], onesrowb_s[:],
                                            b2rb[slot][:, halfd * 512:(halfd + 1) * 512],
                                            start=True, stop=False, skip_group_check=True),
                                 inc=None)
                    for fc in range(32):
                        for th in range(2):
                            bks = (2, 3) if th == 0 else (4, 5)
                            for halfd in range(2):
                                E.op("tensor", lambda eng, fc=fc, th=th, halfd=halfd,
                                     bks=bks, toff=toff:
                                     eng.matmul(pb[bks[halfd]][:],
                                                hcache[:, fc * 512 + toff + th * 128: fc * 512 + toff + (th + 1) * 128],
                                                w2s[:, fc * D + halfd * 512: fc * D + (halfd + 1) * 512],
                                                start=False, stop=(fc == 31),
                                                skip_group_check=True),
                                     inc=(s_pe if (fc == 31 and th == 1 and halfd == 1) else None))
                    ty = s_pe.v

                    E.wait("vector", s_pe, ty)
                    for th in range(2):
                        tile = tg * 2 + th
                        bks = (2, 3) if th == 0 else (4, 5)
                        if slot == 0:
                            if tg >= 1:
                                E.wait("vector", s_st, zstore.get((tg - 1, th), 0))
                            E.op("vector", lambda eng, th=th, bks=bks: eng.tensor_copy(
                                zc[th][:, 0:512], pb[bks[0]][:]), inc=None)
                            E.op("vector", lambda eng, th=th, bks=bks: eng.tensor_copy(
                                zc[th][:, 512:1024], pb[bks[1]][:]), inc=s_dve)
                            tev = s_dve.v
                            E.wait("gpsimd", s_dve, tev)
                            E.op("gpsimd", lambda eng, th=th, tile=tile: eng.dma_start(
                                y0_sc[tile], zc[th][:]), inc=s_st, n=16)
                            zstore[(tg, th)] = s_st.v
                        else:
                            E.wait("sync", s_dve, zdone.get(tile - 1, 0))
                            E.op("sync", lambda eng, th=th, tile=tile: eng.dma_start(
                                y0c[th][:], y0_sc[tile]), inc=s_ld, n=16)
                            E.op("sync", lambda eng, th=th, tile=tile: eng.dma_start(
                                x1c[th][:], x1tm_sc[tile]), inc=s_ld, n=16)
                            tld2 = s_ld.v
                            E.wait("vector", s_ld, tld2)
                            if tg >= 1:
                                E.wait("vector", s_st, zstore.get((tg - 1, th), 0))
                            E.op("vector", lambda eng, th=th, bks=bks: eng.tensor_scalar(
                                zc[th][:, 0:512], pb[bks[0]][:], rtb[:, 3:4], None, ALU.mult),
                                inc=None)
                            E.op("vector", lambda eng, th=th, bks=bks: eng.tensor_scalar(
                                zc[th][:, 512:1024], pb[bks[1]][:], rtb[:, 3:4], None, ALU.mult),
                                inc=s_dve)
                            E.wait("vector", s_dve, s_dve.v)
                            E.op("vector", lambda eng, th=th: eng.scalar_tensor_tensor(
                                zc[th][:], y0c[th][:], rtb[:, 2:3], zc[th][:], ALU.mult, ALU.add),
                                inc=s_dve)
                            E.wait("vector", s_dve, s_dve.v)
                            E.op("vector", lambda eng, th=th: eng.tensor_add(
                                zc[th][:], zc[th][:], x1c[th][:]), inc=s_dve)
                            tz = s_dve.v
                            zdone[tile] = tz
                            E.wait("vector", s_dve, tz)
                            E.op("vector", lambda eng, th=th, tile=tile: eng.reduce_sum(
                                statsM[:, tile:tile + 1], zc[th][:], axis=AX.X), inc=s_dve)
                            E.wait("vector", s_dve, s_dve.v)
                            E.op("vector", lambda eng, tile=tile: eng.tensor_scalar(
                                statsM[:, tile:tile + 1], statsM[:, tile:tile + 1],
                                1.0 / D, None, ALU.mult), inc=s_dve)
                            tmean2 = s_dve.v
                            E.wait("scalar", s_dve, tz)
                            E.op("scalar", lambda eng, th=th, tile=tile: eng.activation(
                                sqjunk[:], zc[th][:], AF.Square,
                                accum_out=statsV[:, tile:tile + 1]), inc=s_act)
                            tsq2 = s_act.v
                            E.wait("vector", s_act, tsq2)
                            E.wait("vector", s_dve, tmean2)
                            E.op("vector", lambda eng, tile=tile: eng.tensor_scalar(
                                statsV[:, tile:tile + 1], statsV[:, tile:tile + 1],
                                1.0 / D, None, ALU.mult), inc=s_dve)
                            E.wait("vector", s_dve, s_dve.v)
                            E.op("vector", lambda eng, tile=tile: eng.scalar_tensor_tensor(
                                statsV[:, tile:tile + 1], statsM[:, tile:tile + 1],
                                statsM[:, tile:tile + 1], statsV[:, tile:tile + 1],
                                ALU.mult, ALU.subtract), inc=s_dve)
                            E.wait("gpsimd", s_dve, tz)
                            E.op("gpsimd", lambda eng, th=th, tile=tile: eng.dma_start(
                                z_sc[tile], zc[th][:]), inc=s_st, n=16)
                            zstore[(tg, th)] = s_st.v
                    ybank_free[tg] = s_dve.v
            slot_pe_done[slot] = s_pe.v
            slot_dve_done[slot] = s_dve.v
            if slot == 0:
                E.wait("gpsimd", s_st, s_st.v)
                mark_slot0 = gp_mark()
            else:
                E.wait("gpsimd", s_st, s_st.v)
                mark_z = gp_mark()

        tstats = s_dve.v
        act_load(SET_LNEXP)
        E.wait("scalar", s_dve, tstats)
        E.op("scalar", lambda eng: eng.activation(
            rstdA[:, 0:ntile], statsV[:, 0:ntile], AF.Ln, bias=epsln_s[:, 0:1],
            scale=-1.0), inc=s_act)
        E.wait("scalar", s_act, s_act.v)
        E.op("scalar", lambda eng: eng.activation(
            rstdB[:, 0:ntile], rstdA[:, 0:ntile], AF.Exp, scale=-0.5), inc=s_act)
        trstd2 = s_act.v
        zrel = {}
        out_st = {}
        for tile in range(ntile):
            if tile == 0:
                E.wait("sync", s_gp, mark_z)
            if tile >= 1:
                E.wait("sync", s_dve, zrel[tile - 1])
            E.op("sync", lambda eng, tile=tile: eng.dma_start(
                zin[tile % 2][:], z_sc[tile]), inc=s_ld, n=16)
            tzl = s_ld.v
            E.wait("vector", s_ld, tzl)
            E.wait("vector", s_act, trstd2)
            if tile >= 2:
                E.wait("vector", s_st, out_st[tile - 2])
            E.op("vector", lambda eng, tile=tile: eng.tensor_scalar(
                oout[tile % 2][:], zin[tile % 2][:], statsM[:, tile:tile + 1],
                rstdB[:, tile:tile + 1], ALU.subtract, ALU.mult), inc=s_dve)
            zrel[tile] = s_dve.v
            E.wait("gpsimd", s_dve, zrel[tile])
            E.op("gpsimd", lambda eng, tile=tile, b=b: eng.dma_start(
                out_d[b, tile * C:(tile + 1) * C, :], oout[tile % 2][:]),
                inc=s_st, n=16)
            out_st[tile] = s_st.v

        E.flush()
        fctx.close()

    E.wait("gpsimd", s_st, s_st.v)
    E.wait("sync", s_st, s_st.v)
    E.flush()
    ctx.close()
    return nc


def kernel(L0_attn_x, L0_attn_Wq, L0_attn_Wk, L0_attn_Wv, L0_attn_Wo, L0_attn_proj,
           L0_ffn_Wr, L0_ffn_W1, L0_ffn_b1, L0_ffn_W2, L0_ffn_b2):
    x = np.ascontiguousarray(np.asarray(L0_attn_x, dtype=np.float32))
    nb_total, T, _ = x.shape
    ncores = 8
    nb = nb_total // ncores
    nch = T // C

    nc = build(nb, nch)

    consts = {
        "idf": np.eye(128, dtype=np.float32),
        "idb": np.eye(128).astype(ml_dtypes.bfloat16),
        "tri": np.triu(np.ones((128, 128))).astype(ml_dtypes.bfloat16),
        "onesrow": np.ones((1, 128), np.float32),
        "onescol": np.ones((128, 1), np.float32),
        "onesrowb": np.ones((1, 128)).astype(ml_dtypes.bfloat16),
        "iota": np.arange(128, dtype=np.float32).reshape(128, 1),
        "iota8": np.arange(NEXP, dtype=np.float32).reshape(1, NEXP),
    }
    shared = {
        "wq": np.ascontiguousarray(np.asarray(L0_attn_Wq, np.float32)),
        "wk": np.ascontiguousarray(np.asarray(L0_attn_Wk, np.float32)),
        "wv": np.ascontiguousarray(np.asarray(L0_attn_Wv, np.float32)),
        "wo": np.ascontiguousarray(np.asarray(L0_attn_Wo, np.float32)),
        "proj": np.ascontiguousarray(np.asarray(L0_attn_proj, np.float32)),
        "wr": np.ascontiguousarray(np.asarray(L0_ffn_Wr, np.float32)),
        "w1": np.ascontiguousarray(np.asarray(L0_ffn_W1, np.float32)),
        "b1": np.ascontiguousarray(np.asarray(L0_ffn_b1, np.float32)),
        "w2": np.ascontiguousarray(np.asarray(L0_ffn_W2, np.float32)),
        "b2": np.ascontiguousarray(np.asarray(L0_ffn_b2, np.float32)),
    }
    in_maps = []
    for core in range(ncores):
        m = {"x": x[core * nb:(core + 1) * nb]}
        m.update(consts)
        m.update(shared)
        in_maps.append(m)
    res = run_bass_kernel_spmd(nc, in_maps, core_ids=list(range(ncores)))
    out = np.concatenate([np.asarray(r["out"]) for r in res.results], axis=0)
    return out.astype(np.float32)


# version 06
# speedup vs baseline: 1.0021x; 1.0021x over previous
import numpy as np
import ml_dtypes
import concourse.bass as bass
import concourse.mybir as mybir
import concourse.bass_utils as _bu
from concourse.bass_utils import run_bass_kernel_spmd
from contextlib import ExitStack

F32 = mybir.dt.float32
BF16 = mybir.dt.bfloat16
U32 = mybir.dt.uint32
AF = mybir.ActivationFunctionType
ALU = mybir.AluOpType
AX = mybir.AxisListType

D = 1024
H = 16
HD = 64
FF = 4096
NEXP = 8
C = 128
GC = 4
EPS_ATTN = 1e-6
EPS_LN = 1e-5

ENGINES = ("sync", "gpsimd", "scalar", "vector", "tensor")
DBG = False


class Sem:
    def __init__(self, nc, name):
        self.h = nc.alloc_semaphore(name)
        self.v = 0

    def inc(self, n=1):
        self.v += n
        return self.v


class Emitter:
    def __init__(self, nc, blk):
        self.nc = nc
        self.blk = blk
        self.q = {e: [] for e in ENGINES}

    def op(self, eng_name, make, inc=None, n=1):
        if inc is None:
            self.q[eng_name].append(make)
            return None
        tgt = inc.inc(n)
        self.q[eng_name].append(lambda eng, mk=make, s=inc.h, nn=n: mk(eng).then_inc(s, nn))
        return tgt

    def wait(self, eng_name, sem, tgt):
        if not tgt:
            return
        self.q[eng_name].append(lambda eng, s=sem.h, t=tgt: eng.wait_ge(s, t))

    def flush(self):
        for name in ENGINES:
            lst = self.q[name]
            if not lst:
                continue
            getattr(self.blk, name)(lambda eng, L=tuple(lst): [f(eng) for f in L])
            self.q[name] = []


def build(nb, nch):
    T = nch * C
    NG = nch // GC
    ntile = nch
    ntg = T // 256

    nc = bass.Bass("TRN2", target_bir_lowering=False)

    x_d = nc.dram_tensor("x", [nb, T, D], F32, kind="ExternalInput").ap()
    wq_d = nc.dram_tensor("wq", [D, D], F32, kind="ExternalInput").ap()
    wk_d = nc.dram_tensor("wk", [D, D], F32, kind="ExternalInput").ap()
    wv_d = nc.dram_tensor("wv", [D, D], F32, kind="ExternalInput").ap()
    wo_d = nc.dram_tensor("wo", [D, D], F32, kind="ExternalInput").ap()
    proj_d = nc.dram_tensor("proj", [HD, HD], F32, kind="ExternalInput").ap()
    wr_d = nc.dram_tensor("wr", [D, NEXP], F32, kind="ExternalInput").ap()
    w1_d = nc.dram_tensor("w1", [NEXP, D, FF], F32, kind="ExternalInput").ap()
    b1_d = nc.dram_tensor("b1", [NEXP, FF], F32, kind="ExternalInput").ap()
    w2_d = nc.dram_tensor("w2", [NEXP, FF, D], F32, kind="ExternalInput").ap()
    b2_d = nc.dram_tensor("b2", [NEXP, D], F32, kind="ExternalInput").ap()
    idf_d = nc.dram_tensor("idf", [128, 128], F32, kind="ExternalInput").ap()
    idb_d = nc.dram_tensor("idb", [128, 128], BF16, kind="ExternalInput").ap()
    tri_d = nc.dram_tensor("tri", [128, 128], BF16, kind="ExternalInput").ap()
    onesrow_d = nc.dram_tensor("onesrow", [1, 128], F32, kind="ExternalInput").ap()
    onescol_d = nc.dram_tensor("onescol", [128, 1], F32, kind="ExternalInput").ap()
    onesrowb_d = nc.dram_tensor("onesrowb", [1, 128], BF16, kind="ExternalInput").ap()
    iota_d = nc.dram_tensor("iota", [128, 1], F32, kind="ExternalInput").ap()
    iota8_d = nc.dram_tensor("iota8", [1, NEXP], F32, kind="ExternalInput").ap()

    out_d = nc.dram_tensor("out", [nb, T, D], F32, kind="ExternalOutput").ap()

    KI = "ExternalOutput" if DBG else "Internal"
    x1fm_sc = nc.dram_tensor("x1fm_sc", [8, 128, T], BF16, kind=KI).ap()
    x1tm_sc = nc.dram_tensor("x1tm_sc", [ntile, 128, D], F32, kind=KI).ap()
    y0_sc = nc.dram_tensor("y0_sc", [ntile, 128, D], BF16, kind=KI).ap()
    z_sc = nc.dram_tensor("z_sc", [ntile, 128, D], F32, kind=KI).ap()
    b2b_sc = nc.dram_tensor("b2b_sc", [2, D], F32, kind=KI).ap()
    attnw_sc = nc.dram_tensor("attnw_sc", [4, 128, 8 * D], BF16, kind="Internal").ap()
    if DBG:
        dbg_d = nc.dram_tensor("dbg", [128, 4096], F32, kind="ExternalOutput").ap()

    ctx = ExitStack()
    sb = lambda n, s, d: ctx.enter_context(nc.sbuf_tensor(n, s, d))
    psb = lambda n, s, d: ctx.enter_context(nc.psum_tensor(n, s, d))

    pb = [psb(f"pb{i}", [128, 512], F32) for i in range(7)]
    pb7b = psb("pb7b", [128, 1024], BF16)

    idf_s = sb("idf_s", [128, 128], F32)
    idb_s = sb("idb_s", [128, 128], BF16)
    tri_s = sb("tri_s", [128, 128], BF16)
    onesrow_s = sb("onesrow_s", [1, 128], F32)
    onescol_s = sb("onescol_s", [128, 1], F32)
    onesrowb_s = sb("onesrowb_s", [1, 128], BF16)
    iota_s = sb("iota_s", [128, 1], F32)
    iota8_s = sb("iota8_s", [1, NEXP], F32)
    epsln_s = sb("epsln_s", [128, 1], F32)
    epsat_s = sb("epsat_s", [128, 1], F32)
    wr_s = sb("wr_s", [128, 8 * NEXP], F32)
    statsM = sb("statsM", [128, 32], F32)
    statsV = sb("statsV", [128, 32], F32)
    rstdA = sb("rstdA", [128, 32], F32)
    rstdB = sb("rstdB", [128, 32], F32)
    rt = sb("rt", [1, 64], F32)
    rtb = sb("rtb", [128, 8], F32)
    idxf = sb("idxf", [128, 80], F32)
    idxu = sb("idxu", [128, 80], U32)
    idxf2 = sb("idxf2", [128, 4], F32)
    idxu2 = sb("idxu2", [128, 4], U32)
    b1s_s = [sb(f"b1s_{s}", [128, 32], F32) for s in range(2)]
    b2rb = [sb(f"b2rb_{s}", [1, D], BF16) for s in range(2)]
    b2g_s = [sb(f"b2g_{s}", [8, 128], F32) for s in range(2)]
    pooled_s = sb("pooled_s", [128, 8], F32)
    logit_s = sb("logit_s", [1, NEXP], F32)
    KVs2 = sb("KVs2", [128, 8 * 130], BF16)
    lnst = sb("lnst", [128, 8], F32)
    sqjunk = sb("sqjunk", [128, 1024], BF16)

    sems = {n: Sem(nc, n) for n in
            ("s_ld", "s_st", "s_wl", "s_pe", "s_dve", "s_act", "s_gp", "s_misc")}
    s_ld, s_st, s_wl = sems["s_ld"], sems["s_st"], sems["s_wl"]
    s_pe, s_dve, s_act = sems["s_pe"], sems["s_dve"], sems["s_act"]
    s_gp, s_misc = sems["s_gp"], sems["s_misc"]

    def gp_mark(queue="gpsimd"):
        t = s_gp.inc(1)
        E.q[queue].append(lambda eng, v=t: eng.sem_inc(s_gp.h, 1))
        return t

    blk = ctx.enter_context(nc.Block())
    E = Emitter(nc, blk)

    from concourse.hw_specs import get_activation_tables
    _tabs = list(get_activation_tables(nc.m.arch).values())
    SET_LNEXP = next(i for i, fs in enumerate(_tabs) if AF.Ln in fs and AF.Exp in fs)
    SET_GELU = next(i for i, fs in enumerate(_tabs) if AF.Gelu in fs)

    def act_load(idx):
        E.op("scalar", lambda eng, idx=idx: eng.add_instruction(
            mybir.InstLoadActFuncSet(
                name=nc.get_next_instruction_name(),
                act_func_set_id=idx, ins=[], outs=[])))

    for t, d in [(idf_s, idf_d), (idb_s, idb_d), (tri_s, tri_d),
                 (onesrow_s, onesrow_d), (onescol_s, onescol_d),
                 (onesrowb_s, onesrowb_d), (iota_s, iota_d), (iota8_s, iota8_d)]:
        E.op("sync", lambda eng, t=t, d=d: eng.dma_start(t[:], d), inc=s_misc, n=16)
    for dc in range(8):
        E.op("sync", lambda eng, dc=dc: eng.dma_start(
            wr_s[:, dc * 8:(dc + 1) * 8], wr_d[dc * 128:(dc + 1) * 128, :]),
            inc=s_ld, n=16)
    t_consts = s_misc.v
    t_wr = s_ld.v
    for e in ("vector", "scalar", "tensor", "gpsimd"):
        E.wait(e, s_misc, t_consts)
    E.op("vector", lambda eng: eng.memset(epsln_s[:], EPS_LN))
    E.op("vector", lambda eng: eng.memset(epsat_s[:], EPS_ATTN), inc=s_dve)
    act_load(SET_LNEXP)

    mark_wstash = [0]
    for b in range(nb):
        actx = ExitStack()
        asb = lambda n, s, d: actx.enter_context(nc.sbuf_tensor(f"{n}_b{b}", s, d))
        wqp = asb("wqp", [128, 8 * D], BF16)
        wkp = asb("wkp", [128, 8 * D], BF16)
        wvp = asb("wvp", [128, 8 * D], BF16)
        wop = asb("wop", [128, 8 * D], BF16)
        wstg = [asb(f"wstg{i}", [128, D], F32) for i in range(2)]
        wstg2 = asb("wstg2", [128, D], BF16)
        wtt = asb("wtt", [128, 128], BF16)
        b2blk = asb("b2blk", [128, 128], BF16)
        projs = asb("projs", [64, 64], F32)
        XR = 4
        xtm = [asb(f"xtm{i}", [128, D], F32) for i in range(XR)]
        xbf = [asb(f"xbf{i}", [128, D], BF16) for i in range(2)]
        xfm = [asb(f"xfm{i}", [128, 8 * 512], BF16) for i in range(2)]
        qf = [asb(f"qf{i}", [128, 8 * 512], BF16) for i in range(2)]
        kf = [asb(f"kf{i}", [128, 8 * 512], BF16) for i in range(2)]
        vaug = [asb("vaug0", [128, 4 * H * 65], BF16)] * 2
        kftm_e = asb("kftm_e", [128, D], BF16)
        kftm_o = asb("kftm_o", [128, D], BF16)
        eluT = asb("eluT", [128, 512], F32)
        eluR = asb("eluR", [128, 512], F32)
        eluE = asb("eluE", [128, 512], F32)
        sm = [asb(f"sm{i}", [128, 128], BF16) for i in range(2)]
        dent = asb("dent", [128, 16], F32)
        attn_cat = [asb("attncat0", [128, D], BF16)] * 2
        attn_fm = [asb("attnfm0", [128, D], BF16)] * 2
        res_s = [asb("res0", [128, D], F32)] * 2
        x1tm = [asb(f"x1tm{i}", [128, D], F32) for i in range(2)]
        x1bf = [asb(f"x1bf{i}", [128, D], BF16) for i in range(2)]
        x1fmg = [asb("x1fmg0", [128, 8 * 512], BF16)] * 2

        kv_started = [False, False]
        pool_scale = 1.0 / T

        qkv_ready = {}
        x_loaded = {}
        x_cast = {}
        xtrans_done = {}
        ln_done = {}
        res_done = {}
        div_done = {}
        smask_last = {}
        kvcopy_done = {}
        kvupd_done = {}
        pool_done = {}
        pool_evac = {}
        x1fm_evac = {}
        x1sp = {}
        group_consumed = {}
        bank7_evac = [0]

        def load_x(ch):
            buf = xtm[ch % XR]
            if ch >= XR:
                E.wait("sync", s_dve, ln_done[ch - XR])
            E.op("sync", lambda eng, buf=buf, ch=ch, b=b: eng.dma_start(
                buf[:], x_d[b, ch * C:(ch + 1) * C, :]), inc=s_ld, n=16)
            x_loaded[ch] = s_ld.v

        def cast_x(ch):
            buf = xbf[ch % 2]
            E.wait("scalar", s_ld, x_loaded[ch])
            if ch >= 2:
                E.wait("scalar", s_pe, xtrans_done[ch - 2])
            E.op("scalar", lambda eng, buf=buf, ch=ch: eng.copy(buf[:], xtm[ch % XR][:]),
                 inc=s_act)
            x_cast[ch] = s_act.v

        def trans_x(ch):
            g = ch // GC
            buf = xbf[ch % 2]
            E.wait("tensor", s_act, x_cast[ch])
            E.wait("tensor", s_dve, bank7_evac[0])
            for dc in range(8):
                E.op("tensor", lambda eng, dc=dc, buf=buf: eng.transpose(
                    pb7b[:, dc * 128:(dc + 1) * 128], buf[:, dc * 128:(dc + 1) * 128],
                    idb_s[:]), inc=(s_pe if dc == 7 else None))
            xtrans_done[ch] = s_pe.v
            E.wait("vector", s_pe, xtrans_done[ch])
            dst = xfm[g % 2]
            tc_ = ch % GC
            E.op("vector", lambda eng, dst=dst, tc_=tc_: eng.tensor_copy(
                dst[:].rearrange("p (dc t) -> p dc t", dc=8)[:, :, tc_ * 128:(tc_ + 1) * 128],
                pb7b[:].rearrange("p (dc t) -> p dc t", dc=8)), inc=s_dve)
            bank7_evac[0] = s_dve.v

        def proj_group(g):
            xg = xfm[g % 2]
            qd, kd, vd = qf[g % 2], kf[g % 2], vaug[g % 2]
            E.wait("tensor", s_dve, s_dve.v)
            if g >= 1:
                E.wait("vector", s_pe, group_consumed[g - 1])
            E.op("vector", lambda eng, vd=vd: eng.memset(vd[:], 1.0), inc=s_dve)
            prev_add = 0
            elu_min = {}
            for wi, (wsrc, dst, bks) in enumerate(((wqp, qd, (2, 3)), (wkp, kd, (4, 5)))):
                for pc in range(8):
                    bk = bks[pc % 2]
                    if pc >= 2:
                        E.wait("tensor", s_dve, elu_min[(wi, pc - 2)])
                    for dc in range(8):
                        E.op("tensor", lambda eng, wsrc=wsrc, dc=dc, pc=pc, bk=bk, xg=xg:
                             eng.matmul(pb[bk][:],
                                        wsrc[:, dc * D + pc * 128:dc * D + (pc + 1) * 128],
                                        xg[:, dc * 512:(dc + 1) * 512],
                                        start=(dc == 0), stop=(dc == 7)),
                             inc=(s_pe if dc == 7 else None))
                    tmm = s_pe.v
                    E.wait("scalar", s_pe, tmm)
                    E.wait("scalar", s_dve, prev_add)
                    E.op("scalar", lambda eng, bk=bk: eng.activation(
                        eluR[:], pb[bk][:], AF.Relu), inc=s_act)
                    trelu = s_act.v
                    E.wait("vector", s_act, trelu)
                    E.op("vector", lambda eng, bk=bk: eng.tensor_scalar(
                        eluT[:], pb[bk][:], 0.0, None, ALU.min), inc=s_dve)
                    elu_min[(wi, pc)] = s_dve.v
                    E.wait("scalar", s_dve, elu_min[(wi, pc)])
                    E.op("scalar", lambda eng: eng.activation(
                        eluE[:], eluT[:], AF.Exp), inc=s_act)
                    texp = s_act.v
                    E.wait("vector", s_act, texp)
                    E.op("vector", lambda eng, dst=dst, pc=pc: eng.tensor_add(
                        dst[:, pc * 512:(pc + 1) * 512], eluE[:], eluR[:]), inc=s_dve)
                    prev_add = s_dve.v
            prev_vev = 0
            for tc_ in range(4):
                bka, bkb = (2, 3) if tc_ % 2 == 0 else (4, 5)
                if tc_ < 2:
                    E.wait("tensor", s_dve, elu_min[(1, 6 + tc_)])
                else:
                    E.wait("tensor", s_dve, prev_vev)
                for dc in range(8):
                    E.op("tensor", lambda eng, dc=dc, tc_=tc_, bka=bka, xg=xg:
                         eng.matmul(pb[bka][:],
                                    xg[:, dc * 512 + tc_ * 128:dc * 512 + (tc_ + 1) * 128],
                                    wvp[:, dc * D:dc * D + 512],
                                    start=(dc == 0), stop=(dc == 7)), inc=None)
                    E.op("tensor", lambda eng, dc=dc, tc_=tc_, bkb=bkb, xg=xg:
                         eng.matmul(pb[bkb][:],
                                    xg[:, dc * 512 + tc_ * 128:dc * 512 + (tc_ + 1) * 128],
                                    wvp[:, dc * D + 512:(dc + 1) * D],
                                    start=(dc == 0), stop=(dc == 7)),
                         inc=(s_pe if dc == 7 else None))
                tv = s_pe.v
                E.wait("vector", s_pe, tv)
                E.op("vector", lambda eng, vd=vd, tc_=tc_, bka=bka: eng.tensor_copy(
                    vd[:].rearrange("p (t h e) -> p t h e", t=4, h=H)[:, tc_, 0:8, 0:64],
                    pb[bka][:].rearrange("p (h e) -> p h e", h=8)), inc=None)
                E.op("vector", lambda eng, vd=vd, tc_=tc_, bkb=bkb: eng.tensor_copy(
                    vd[:].rearrange("p (t h e) -> p t h e", t=4, h=H)[:, tc_, 8:16, 0:64],
                    pb[bkb][:].rearrange("p (h e) -> p h e", h=8)), inc=s_dve)
                prev_vev = s_dve.v
            qkv_ready[g] = s_dve.v

        def outaug_region(h):
            p = h // 2
            if p < 3:
                return 4, p * 130 + (h % 2) * 65
            if p < 6:
                return 5, (p - 3) * 130 + (h % 2) * 65
            return 6, (p - 6) * 130 + (h % 2) * 65

        def scan_chunk(ch):
            g = ch // GC
            tc_ = ch % GC
            qd, kd, vd = qf[g % 2], kf[g % 2], vaug[g % 2]
            par = ch % 2
            E.wait("tensor", s_dve, qkv_ready[g])
            E.wait("tensor", s_dve, bank7_evac[0])
            for dc in range(8):
                E.op("tensor", lambda eng, dc=dc, kd=kd, tc_=tc_: eng.transpose(
                    pb7b[:, dc * 128:(dc + 1) * 128],
                    kd[:, dc * 512 + tc_ * 128:dc * 512 + (tc_ + 1) * 128], idb_s[:]),
                    inc=(s_pe if dc == 7 else None))
            tkt = s_pe.v
            E.wait("vector", s_pe, tkt)
            if ch == 0:
                E.op("vector", lambda eng: eng.memset(pooled_s[:], 0.0))
                E.op("vector", lambda eng: eng.memset(kftm_e[:], 0.0))
                E.op("vector", lambda eng: eng.memset(kftm_o[:], 0.0))
                E.op("vector", lambda eng: eng.memset(KVs2[:], 0.0))
            E.op("vector", lambda eng: eng.tensor_copy(
                kftm_e[:].rearrange("p (hc d) -> p hc d", hc=8)[:, :, 0:64],
                pb7b[:].rearrange("p (hc d) -> p hc d", hc=8)[:, :, 0:64]), inc=None)
            E.op("vector", lambda eng: eng.tensor_copy(
                kftm_o[:].rearrange("p (hc d) -> p hc d", hc=8)[:, :, 64:128],
                pb7b[:].rearrange("p (hc d) -> p hc d", hc=8)[:, :, 64:128]), inc=s_dve)
            bank7_evac[0] = s_dve.v
            tkf_evac = s_dve.v

            E.wait("vector", s_pe, kvupd_done.get(ch - 1, 0))
            if ch == 0:
                E.op("vector", lambda eng: eng.memset(lnst[:, 7:8], 0.0), inc=s_dve)
            else:
                E.op("vector", lambda eng: eng.tensor_copy(
                    KVs2[0:64, 0:910].rearrange("p (pr c) -> p pr c", c=130)[:, :, 0:65],
                    pb[0][0:64, 0:455].rearrange("p (pr c) -> p pr c", c=65)), inc=None)
                E.op("vector", lambda eng: eng.tensor_copy(
                    KVs2[0:64, 910:975], pb[1][0:64, 0:65]), inc=None)
                E.op("vector", lambda eng: eng.tensor_copy(
                    KVs2[64:128, 0:910].rearrange("p (pr c) -> p pr c", c=130)[:, :, 65:130],
                    pb[0][64:128, 0:455].rearrange("p (pr c) -> p pr c", c=65)), inc=None)
                E.op("vector", lambda eng: eng.tensor_copy(
                    KVs2[64:128, 975:1040], pb[1][64:128, 0:65]), inc=s_dve)
            kvcopy_done[ch] = s_dve.v

            E.wait("tensor", s_dve, kvcopy_done[ch])
            E.wait("tensor", s_dve, res_done.get(ch - 1, 0))
            E.wait("tensor", s_dve, pool_evac.get(ch - 1, 0))
            smask = {}
            outp = {}

            def head_slices(h):
                pq = (h % 2) * 64
                hc = h // 2
                qsl = qd[pq:pq + 64, hc * 512 + tc_ * 128: hc * 512 + (tc_ + 1) * 128]
                ksl = kd[pq:pq + 64, hc * 512 + tc_ * 128: hc * 512 + (tc_ + 1) * 128]
                vsl = vd[:, tc_ * H * 65 + h * 65: tc_ * H * 65 + (h + 1) * 65]
                qpair = qd[:, hc * 512 + tc_ * 128: hc * 512 + (tc_ + 1) * 128]
                kvz = KVs2[:, hc * 130:(hc + 1) * 130]
                return qsl, ksl, vsl, qpair, kvz

            def emit_S(h):
                qsl, ksl, _, _, _ = head_slices(h)
                sbk = 2 + (h % 2)
                if h >= 2:
                    E.wait("tensor", s_dve, smask[h - 2])
                E.op("tensor", lambda eng, sbk=sbk, ksl=ksl, qsl=qsl: eng.matmul(
                    pb[sbk][:, 0:128], ksl, qsl, start=True, stop=True), inc=s_pe)
                tS = s_pe.v
                E.wait("vector", s_pe, tS)
                if h >= 2:
                    E.wait("vector", s_pe, outp[h - 2])
                E.op("vector", lambda eng, sbk=sbk, h=h: eng.tensor_mul(
                    sm[h % 2][:], pb[sbk][:, 0:128], tri_s[:]), inc=s_dve)
                smask[h] = s_dve.v

            def emit_outaug(h):
                _, _, vsl, qpair, kvz = head_slices(h)
                obk, off = outaug_region(h)
                E.wait("tensor", s_dve, smask[h])
                fib = h in (0, 6, 12)
                E.op("tensor", lambda eng, obk=obk, off=off, h=h, vsl=vsl, fib=fib:
                     eng.matmul(pb[obk][:, off:off + 65], sm[h % 2][:], vsl,
                                start=fib, stop=False, skip_group_check=True), inc=s_pe)
                outp[h] = s_pe.v
                if h % 2 == 1:
                    poff = off - 65
                    E.op("tensor", lambda eng, obk=obk, poff=poff, qpair=qpair, kvz=kvz:
                         eng.matmul(pb[obk][:, poff:poff + 130], qpair, kvz,
                                    start=False, stop=True, skip_group_check=True),
                         inc=s_pe)
                    outp[h] = s_pe.v

            emit_S(0)
            for h in range(H):
                if h + 1 < H:
                    emit_S(h + 1)
                emit_outaug(h)
            smask_last[ch] = smask[H - 1]

            E.wait("tensor", s_dve, tkf_evac)
            for h in range(H):
                hc = h // 2
                kvb = 0 if hc < 7 else 1
                off = hc * 65 if hc < 7 else 0
                vsl = vd[:, tc_ * H * 65 + h * 65: tc_ * H * 65 + (h + 1) * 65]
                ksl_tm = (kftm_e if h % 2 == 0 else kftm_o)[:, hc * 128:(hc + 1) * 128]
                st = (ch == 0) and (not kv_started[kvb])
                if st:
                    kv_started[kvb] = True
                E.op("tensor", lambda eng, kvb=kvb, off=off, ksl_tm=ksl_tm,
                     vsl=vsl, st=st, ch=ch:
                     eng.matmul(pb[kvb][:, off:off + 65], ksl_tm, vsl,
                                start=st, stop=(ch == nch - 1), skip_group_check=True),
                     inc=(s_pe if h == H - 1 else None))
            kvupd_done[ch] = s_pe.v

            cat = attn_cat[par]
            E.wait("vector", s_pe, outp[H - 1])
            E.op("vector", lambda eng: eng.tensor_scalar(
                dent[:, 0:6].rearrange("p (h e) -> p h e", e=1),
                pb[4][:, 0:390].rearrange("p (h e) -> p h e", e=65)[:, :, 64:65],
                epsat_s[:, 0:1], None, ALU.add), inc=None)
            E.op("vector", lambda eng: eng.tensor_scalar(
                dent[:, 6:12].rearrange("p (h e) -> p h e", e=1),
                pb[5][:, 0:390].rearrange("p (h e) -> p h e", e=65)[:, :, 64:65],
                epsat_s[:, 0:1], None, ALU.add), inc=None)
            E.op("vector", lambda eng: eng.tensor_scalar(
                dent[:, 12:16].rearrange("p (h e) -> p h e", e=1),
                pb[6][:, 0:260].rearrange("p (h e) -> p h e", e=65)[:, :, 64:65],
                epsat_s[:, 0:1], None, ALU.add), inc=s_dve)
            tden = s_dve.v
            E.wait("vector", s_dve, tden)
            E.op("vector", lambda eng: eng.reciprocal(dent[:, 0:16], dent[:, 0:16]),
                 inc=s_dve)
            trec = s_dve.v
            E.wait("vector", s_dve, trec)
            for h in range(H):
                obk, off = outaug_region(h)
                E.op("vector", lambda eng, obk=obk, off=off, h=h, cat=cat:
                     eng.tensor_scalar(cat[:, h * 64:(h + 1) * 64],
                                       pb[obk][:, off:off + 64],
                                       dent[:, h:h + 1], None, ALU.mult),
                     inc=(s_dve if h == H - 1 else None))
            div_done[ch] = s_dve.v

            if DBG and ch == -1 and b == 0:
                E.wait("vector", s_dve, div_done[ch])
                E.op("vector", lambda eng: eng.tensor_copy(sqjunk[:], cat[:]), inc=s_dve)
                tdbg = s_dve.v
                E.wait("gpsimd", s_dve, tdbg)
                E.op("gpsimd", lambda eng: eng.dma_start(dbg_d[:, 0:1024], sqjunk[:]),
                     inc=s_st, n=16)
                E.op("gpsimd", lambda eng: eng.dma_start(dbg_d[:, 1024:1040], dent[:]),
                     inc=s_st, n=16)
                E.op("gpsimd", lambda eng, vd=vd: eng.dma_start(
                    dbg_d[:, 1040:1105].rearrange("p e -> p e"),
                    vd[:, 0:65]), inc=s_st, n=16)
                E.op("gpsimd", lambda eng: eng.dma_start(
                    dbg_d[:, 1105:1625], KVs_e[:]), inc=s_st, n=16)

            E.wait("tensor", s_dve, div_done[ch])
            E.wait("tensor", s_dve, bank7_evac[0])
            for dc in range(8):
                E.op("tensor", lambda eng, dc=dc, cat=cat: eng.transpose(
                    pb7b[:, dc * 128:(dc + 1) * 128], cat[:, dc * 128:(dc + 1) * 128],
                    idb_s[:]), inc=(s_pe if dc == 7 else None))
            tat = s_pe.v
            E.wait("vector", s_pe, tat)
            E.op("vector", lambda eng, par=par: eng.tensor_copy(attn_fm[par][:], pb7b[:]),
                 inc=s_dve)
            bank7_evac[0] = s_dve.v
            tafm = s_dve.v

            E.wait("tensor", s_dve, tafm)
            for half, bk in ((0, 4), (1, 5)):
                for hc2 in range(8):
                    E.op("tensor", lambda eng, hc2=hc2, half=half, bk=bk, par=par:
                         eng.matmul(pb[bk][:],
                                    attn_fm[par][:, hc2 * 128:(hc2 + 1) * 128],
                                    wop[:, hc2 * D + half * 512: hc2 * D + (half + 1) * 512],
                                    start=(hc2 == 0), stop=(hc2 == 7)),
                         inc=(s_pe if (hc2 == 7 and half == 1) else None))
            two = s_pe.v

            xres = xtm[ch % XR]
            rr = res_s[par]
            E.wait("vector", s_pe, two)
            E.op("vector", lambda eng, rr=rr, xres=xres: eng.tensor_add(
                rr[:, 0:512], pb[4][:], xres[:, 0:512]), inc=None)
            E.op("vector", lambda eng, rr=rr, xres=xres: eng.tensor_add(
                rr[:, 512:1024], pb[5][:], xres[:, 512:1024]), inc=s_dve)
            tres = s_dve.v
            res_done[ch] = tres
            E.wait("vector", s_dve, tres)
            E.op("vector", lambda eng, rr=rr: eng.reduce_sum(
                lnst[:, 0:1], rr[:], axis=AX.X), inc=s_dve)
            tsum = s_dve.v
            E.wait("vector", s_dve, tsum)
            E.op("vector", lambda eng: eng.tensor_scalar(
                lnst[:, 1:2], lnst[:, 0:1], 1.0 / D, None, ALU.mult), inc=s_dve)
            tmean = s_dve.v
            E.wait("scalar", s_dve, tres)
            E.op("scalar", lambda eng, rr=rr: eng.activation(
                sqjunk[:], rr[:], AF.Square, accum_out=lnst[:, 2:3]), inc=s_act)
            tsq = s_act.v
            E.wait("vector", s_act, tsq)
            E.op("vector", lambda eng: eng.tensor_scalar(
                lnst[:, 3:4], lnst[:, 2:3], 1.0 / D, None, ALU.mult), inc=s_dve)
            tmsq = s_dve.v
            E.wait("vector", s_dve, tmsq)
            E.op("vector", lambda eng: eng.scalar_tensor_tensor(
                lnst[:, 4:5], lnst[:, 1:2], lnst[:, 1:2], lnst[:, 3:4],
                ALU.mult, ALU.subtract), inc=s_dve)
            tnv = s_dve.v
            E.wait("scalar", s_dve, tnv)
            E.op("scalar", lambda eng: eng.activation(
                lnst[:, 5:6], lnst[:, 4:5], AF.Ln, bias=epsln_s[:, 0:1], scale=-1.0),
                inc=s_act)
            tln = s_act.v
            E.wait("scalar", s_act, tln)
            E.op("scalar", lambda eng: eng.activation(
                lnst[:, 6:7], lnst[:, 5:6], AF.Exp, scale=-0.5), inc=s_act)
            trstd = s_act.v
            E.wait("vector", s_act, trstd)
            if ch >= 2:
                E.wait("vector", s_st, x1sp[ch - 2])
            x1t = x1tm[par]
            E.op("vector", lambda eng, rr=rr, x1t=x1t: eng.tensor_scalar(
                x1t[:], rr[:], lnst[:, 1:2], lnst[:, 6:7], ALU.subtract, ALU.mult),
                inc=s_dve)
            ln_done[ch] = s_dve.v

            if DBG and ch == 0 and b == 0:
                E.wait("gpsimd", s_dve, ln_done[ch])
                E.op("gpsimd", lambda eng, rr=rr: eng.dma_start(
                    dbg_d[:, 2048:3072], rr[:]), inc=s_st, n=16)
                E.op("gpsimd", lambda eng: eng.dma_start(
                    dbg_d[:, 3072:3080], lnst[:]), inc=s_st, n=16)

            E.wait("tensor", s_dve, ln_done[ch])
            for dc in range(8):
                E.op("tensor", lambda eng, dc=dc, x1t=x1t: eng.matmul(
                    pb[6][:, 384 + dc:385 + dc], x1t[:, dc * 128:(dc + 1) * 128],
                    onescol_s[:], start=(dc == 0), stop=(dc == 7),
                    skip_group_check=True),
                    inc=(s_pe if dc == 7 else None))
            pool_done[ch] = s_pe.v
            E.wait("vector", s_pe, pool_done[ch])
            E.op("vector", lambda eng: eng.tensor_add(
                pooled_s[:], pooled_s[:], pb[6][:, 384:392]), inc=s_dve)
            pool_evac[ch] = s_dve.v

            E.wait("gpsimd", s_dve, ln_done[ch])
            E.op("gpsimd", lambda eng, x1t=x1t, ch=ch: eng.dma_start(
                x1tm_sc[ch], x1t[:]), inc=s_st, n=16)
            x1sp[ch] = s_st.v
            E.wait("scalar", s_dve, ln_done[ch])
            xb1 = x1bf[par]
            E.op("scalar", lambda eng, xb1=xb1, x1t=x1t: eng.copy(xb1[:], x1t[:]),
                 inc=s_act)
            tx1c = s_act.v
            E.wait("tensor", s_act, tx1c)
            E.wait("tensor", s_dve, bank7_evac[0])
            for dc in range(8):
                E.op("tensor", lambda eng, dc=dc, xb1=xb1: eng.transpose(
                    pb7b[:, dc * 128:(dc + 1) * 128], xb1[:, dc * 128:(dc + 1) * 128],
                    idb_s[:]), inc=(s_pe if dc == 7 else None))
            tx1t = s_pe.v
            E.wait("vector", s_pe, tx1t)
            if tc_ == 0 and g >= 1:
                E.wait("vector", s_st, x1sp[("fm", g - 1)])
            gdst = x1fmg[g % 2]
            E.op("vector", lambda eng, gdst=gdst, tc_=tc_: eng.tensor_copy(
                gdst[:].rearrange("p (dc t) -> p dc t", dc=8)[:, :, tc_ * 128:(tc_ + 1) * 128],
                pb7b[:].rearrange("p (dc t) -> p dc t", dc=8)), inc=s_dve)
            bank7_evac[0] = s_dve.v
            x1fm_evac[ch] = s_dve.v
            if tc_ == GC - 1:
                E.wait("gpsimd", s_dve, x1fm_evac[ch])
                E.op("gpsimd", lambda eng, gdst=gdst, g=g: eng.dma_start(
                    x1fm_sc[:, :, g * 512:(g + 1) * 512].rearrange("dc p t -> p dc t"),
                    gdst[:].rearrange("p (dc t) -> p dc t", dc=8)), inc=s_st, n=16)
                x1sp[("fm", g)] = s_st.v
                group_consumed[g] = s_pe.v

        for ch in range(min(XR, nch)):
            load_x(ch)
        for ch in range(min(GC, nch)):
            cast_x(ch)
            trans_x(ch)

        if b == 0:
            E.op("sync", lambda eng: eng.dma_start(projs[:], proj_d), inc=s_ld, n=16)
            tpl = s_ld.v
            E.wait("vector", s_ld, tpl)
            E.op("vector", lambda eng: eng.memset(b2blk[:], 0.0))
            E.op("vector", lambda eng: eng.tensor_copy(b2blk[0:64, 0:64], projs[:]))
            E.op("vector", lambda eng: eng.tensor_copy(b2blk[64:128, 64:128], projs[:]),
                 inc=s_dve)
            t_b2 = s_dve.v

            wload = {}
            plan = []
            for wi, (wsrc, wdst, premul) in enumerate(
                    ((wq_d, wqp, True), (wk_d, wkp, True),
                     (wv_d, wvp, False), (wo_d, wop, False))):
                for dc in range(8):
                    plan.append((wi, wsrc, wdst, premul, dc))
            cast_done = {}

            def emit_wload(k):
                if k >= len(plan):
                    return
                _, wsrc, _, _, dc = plan[k]
                if k >= 2:
                    E.wait("sync", s_act, cast_done[k - 2])
                E.op("sync", lambda eng, wsrc=wsrc, dc=dc, k=k: eng.dma_start(
                    wstg[k % 2][:], wsrc[dc * 128:(dc + 1) * 128, :]), inc=s_ld, n=16)
                wload[k] = s_ld.v

            emit_wload(0)
            prev_tr_last = 0
            prev_evac = bank7_evac[0]
            prev_mm = 0
            prev_wev = 0
            for k, (wi, wsrc, wdst, premul, dc) in enumerate(plan):
                emit_wload(k + 1)
                E.wait("scalar", s_ld, wload[k])
                if not premul:
                    E.op("scalar", lambda eng, wdst=wdst, dc=dc, k=k: eng.copy(
                        wdst[:, dc * D:(dc + 1) * D], wstg[k % 2][:]), inc=s_act)
                    cast_done[k] = s_act.v
                    continue
                E.wait("scalar", s_pe, prev_tr_last)
                E.op("scalar", lambda eng, k=k: eng.copy(wstg2[:], wstg[k % 2][:]),
                     inc=s_act)
                cast_done[k] = s_act.v
                for qc in range(8):
                    E.wait("tensor", s_act, cast_done[k])
                    E.wait("tensor", s_dve, prev_evac)
                    E.op("tensor", lambda eng, qc=qc: eng.transpose(
                        pb7b[:, 0:128], wstg2[:, qc * 128:(qc + 1) * 128], idb_s[:]),
                        inc=s_pe)
                    ttr = s_pe.v
                    if qc == 7:
                        prev_tr_last = ttr
                    E.wait("vector", s_pe, ttr)
                    E.wait("vector", s_pe, prev_mm)
                    E.op("vector", lambda eng: eng.tensor_copy(wtt[:], pb7b[:, 0:128]),
                         inc=s_dve)
                    prev_evac = s_dve.v
                    E.wait("tensor", s_dve, prev_evac)
                    E.wait("tensor", s_dve, t_b2)
                    E.wait("tensor", s_act, prev_wev)
                    E.op("tensor", lambda eng: eng.matmul(
                        pb[6][:, 0:128], wtt[:], b2blk[:], start=True, stop=True),
                        inc=s_pe)
                    prev_mm = s_pe.v
                    E.wait("scalar", s_pe, prev_mm)
                    E.op("scalar", lambda eng, wdst=wdst, dc=dc, qc=qc: eng.copy(
                        wdst[:, dc * D + qc * 128: dc * D + (qc + 1) * 128],
                        pb[6][:, 0:128]), inc=s_act)
                    prev_wev = s_act.v
            bank7_evac[0] = prev_evac
            t_built = s_act.v
            E.wait("gpsimd", s_act, t_built)
            for wi, wsb in enumerate((wqp, wkp, wvp, wop)):
                E.op("gpsimd", lambda eng, wi=wi, wsb=wsb: eng.dma_start(
                    attnw_sc[wi], wsb[:]), inc=s_st, n=16)
            E.wait("gpsimd", s_st, s_st.v)
            gmark = s_gp.inc(1)
            E.q["gpsimd"].append(lambda eng: eng.sem_inc(s_gp.h, 1))
            mark_wstash[0] = gmark
        else:
            E.wait("sync", s_gp, mark_wstash[0])
            for wi, wsb in enumerate((wqp, wkp, wvp, wop)):
                E.op("sync", lambda eng, wi=wi, wsb=wsb: eng.dma_start(
                    wsb[:], attnw_sc[wi]), inc=s_ld, n=16)
            t_wre = s_ld.v
            E.wait("tensor", s_ld, t_wre)
            E.wait("scalar", s_ld, t_wre)

        for g in range(NG):
            proj_group(g)
            for ch in range(g * GC, (g + 1) * GC):
                scan_chunk(ch)
            for ch in range((g + 1) * GC, min((g + 2) * GC, nch)):
                if ch >= XR:
                    load_x(ch)
                cast_x(ch)
                trans_x(ch)

        E.wait("vector", s_dve, pool_evac[nch - 1])
        E.op("vector", lambda eng: eng.tensor_scalar(
            pooled_s[:], pooled_s[:], pool_scale, None, ALU.mult), inc=s_dve)
        tpooled = s_dve.v
        E.wait("tensor", s_dve, tpooled)
        for dc in range(8):
            E.op("tensor", lambda eng, dc=dc: eng.matmul(
                pb[6][0:1, 0:8], pooled_s[:, dc:dc + 1], wr_s[:, dc * 8:(dc + 1) * 8],
                start=(dc == 0), stop=(dc == 7)), inc=(s_pe if dc == 7 else None))
        tlog = s_pe.v
        E.wait("vector", s_pe, tlog)

        def vsedge(make):
            E.op("vector", make, inc=s_dve)
            E.wait("vector", s_dve, s_dve.v)

        vsedge(lambda eng: eng.tensor_copy(logit_s[:], pb[6][0:1, 0:8]))
        vsedge(lambda eng: eng.reduce_max(rt[0:1, 0:1], logit_s[:], axis=AX.X))
        vsedge(lambda eng: eng.tensor_scalar(rt[0:1, 16:24], logit_s[:], rt[0:1, 0:1],
                                             None, ALU.subtract))
        vsedge(lambda eng: eng.tensor_scalar(rt[0:1, 24:32], rt[0:1, 16:24], 0.0,
                                             None, ALU.is_ge))
        vsedge(lambda eng: eng.tensor_mul(rt[0:1, 32:40], rt[0:1, 24:32], iota8_s[:]))
        vsedge(lambda eng: eng.reduce_sum(rt[0:1, 2:3], rt[0:1, 32:40], axis=AX.X))
        vsedge(lambda eng: eng.scalar_tensor_tensor(rt[0:1, 40:48], rt[0:1, 24:32],
                                                    1e30, logit_s[:], ALU.mult,
                                                    ALU.subtract))
        vsedge(lambda eng: eng.tensor_reduce(rt[0:1, 1:2], rt[0:1, 40:48], AX.X,
                                             ALU.min, negate=True))
        vsedge(lambda eng: eng.tensor_scalar(rt[0:1, 48:56], logit_s[:], rt[0:1, 1:2],
                                             None, ALU.is_ge))
        vsedge(lambda eng: eng.tensor_sub(rt[0:1, 56:64], rt[0:1, 48:56], rt[0:1, 24:32]))
        vsedge(lambda eng: eng.tensor_mul(rt[0:1, 32:40], rt[0:1, 56:64], iota8_s[:]))
        vsedge(lambda eng: eng.reduce_sum(rt[0:1, 3:4], rt[0:1, 32:40], axis=AX.X))
        vsedge(lambda eng: eng.tensor_scalar(rt[0:1, 4:5], rt[0:1, 0:1], -1.0,
                                             None, ALU.mult))
        tprev = s_dve.v
        E.wait("scalar", s_dve, tprev)
        E.op("scalar", lambda eng: eng.activation(rt[0:1, 5:6], rt[0:1, 1:2], AF.Exp,
                                                  bias=rt[0:1, 4:5]), inc=s_act)
        E.wait("vector", s_act, s_act.v)
        vsedge(lambda eng: eng.tensor_scalar(rt[0:1, 6:7], rt[0:1, 5:6], 1.0,
                                             None, ALU.add))
        vsedge(lambda eng: eng.reciprocal(rt[0:1, 7:8], rt[0:1, 6:7]))
        vsedge(lambda eng: eng.tensor_mul(rt[0:1, 8:9], rt[0:1, 5:6], rt[0:1, 7:8]))
        trt = s_dve.v

        E.wait("tensor", s_dve, trt)
        for i, col in enumerate((2, 3, 7, 8)):
            E.op("tensor", lambda eng, i=i, col=col: eng.matmul(
                pb[6][:, 128 + i:129 + i], onesrow_s[:], rt[0:1, col:col + 1],
                start=True, stop=True, skip_group_check=True),
                inc=(s_pe if i == 3 else None))
        tbc = s_pe.v
        E.wait("vector", s_pe, tbc)
        vsedge(lambda eng: eng.tensor_scalar(rtb[:, 0:2], pb[6][:, 128:130], 1024.0,
                                             None, ALU.mult))
        vsedge(lambda eng: eng.tensor_copy(rtb[:, 2:4], pb[6][:, 130:132]))
        vsedge(lambda eng: eng.tensor_scalar(rtb[:, 4:6], pb[6][:, 128:130], 4096.0,
                                             None, ALU.mult))
        vsedge(lambda eng: eng.tensor_scalar(rtb[:, 6:8], pb[6][:, 128:130], 32.0,
                                             None, ALU.mult))
        t_rtb = s_dve.v
        for s in range(2):
            for dc in range(8):
                E.op("vector", lambda eng, s=s, dc=dc: eng.tensor_scalar(
                    idxf[:, s * 8 + dc: s * 8 + dc + 1], iota_s[:],
                    rtb[:, s:s + 1], float(dc * 128), ALU.add, ALU.add), inc=None)
            for fc in range(32):
                E.op("vector", lambda eng, s=s, fc=fc: eng.tensor_scalar(
                    idxf[:, 16 + s * 32 + fc: 17 + s * 32 + fc], iota_s[:],
                    rtb[:, 4 + s:5 + s], float(fc * 128), ALU.add, ALU.add), inc=None)
            E.op("vector", lambda eng, s=s: eng.tensor_scalar(
                idxf2[:, s:s + 1], iota_s[:], rtb[:, 6 + s:7 + s], None, ALU.add),
                inc=None)
            E.op("vector", lambda eng, s=s: eng.tensor_scalar(
                idxf2[:, 2 + s:3 + s], rtb[:, 6 + s:7 + s], 0.25, None, ALU.mult),
                inc=s_dve)
        tidxf = s_dve.v
        E.wait("vector", s_dve, tidxf)
        E.op("vector", lambda eng: eng.tensor_scalar(
            idxf2[:, 2:4], idxf2[:, 2:4], iota_s[:, 0:1], None, ALU.add), inc=s_dve)
        tidxf2 = s_dve.v
        E.wait("vector", s_dve, tidxf2)
        E.op("vector", lambda eng: eng.tensor_copy(idxu[:], idxf[:]), inc=None)
        E.op("vector", lambda eng: eng.tensor_copy(idxu2[:], idxf2[:]), inc=s_dve)
        tidx = s_dve.v

        E.wait("gpsimd", s_st, s_st.v)
        mark_attn = gp_mark()
        E.wait("gpsimd", s_dve, tidx)
        for s in range(2):
            E.op("gpsimd", lambda eng, s=s: eng.indirect_dma_start(
                b2g_s[s][:], None, b2_d.rearrange("e (r f) -> e r f", f=128),
                bass.IndirectOffsetOnAxis(idxu2[0:8, 2 + s:3 + s], 1)), inc=s_misc, n=16)
            tb2g = s_misc.v
            E.wait("gpsimd", s_misc, tb2g)
            E.op("gpsimd", lambda eng, s=s: eng.dma_start(
                b2b_sc[s].rearrange("(p f) -> p f", f=128), b2g_s[s][:]),
                inc=s_misc, n=16)
        E.wait("gpsimd", s_misc, s_misc.v)
        for s in range(2):
            E.op("gpsimd", lambda eng, s=s: eng.dma_start(
                b2rb[s][:], b2b_sc[s:s + 1, :]), inc=s_misc, n=16)
        E.wait("gpsimd", s_misc, s_misc.v)
        mark_b2 = gp_mark()

        E.flush()
        actx.close()

        fctx = ExitStack()
        fsb = lambda n, s, d: fctx.enter_context(nc.sbuf_tensor(f"{n}_b{b}", s, d))
        b1r = [fsb(f"b1r_{s}", [32, 128], F32) for s in range(2)]
        w1s = fsb("w1s", [128, 8 * FF], BF16)
        w2s = fsb("w2s", [128, 32 * D], BF16)
        x1t_f = [fsb(f"x1t_f{i}", [128, 8 * 512], BF16) for i in range(2)]
        hcache = fsb("hcache", [128, 32 * 512], BF16)
        y0c = [fsb(f"y0c{i}", [128, D], BF16) for i in range(2)]
        x1c = [fsb("x1c0", [128, D], F32)] * 2
        zc = [fsb(f"zc{i}", [128, D], F32) for i in range(2)]
        zin = x1c
        oout = zc

        tb2cast = None

        E.wait("gpsimd", s_dve, tidx)
        for s in range(2):
            E.op("gpsimd", lambda eng, s=s: eng.indirect_dma_start(
                b1r[s][:], None, b1_d.rearrange("e (r f) -> e r f", f=128),
                bass.IndirectOffsetOnAxis(idxu2[0:32, s:s + 1], 1)), inc=s_misc, n=16)
        E.wait("gpsimd", s_misc, s_misc.v)
        mark_b1 = gp_mark()
        E.wait("tensor", s_gp, mark_b1)
        E.wait("tensor", s_dve, t_rtb)
        for s in range(2):
            E.op("tensor", lambda eng, s=s: eng.transpose(
                pb[6][:, 128 + s * 32:160 + s * 32], b1r[s][:], idf_s[0:32, 0:32]), inc=s_pe)
        tb1t = s_pe.v
        E.wait("vector", s_pe, tb1t)
        for s in range(2):
            E.op("vector", lambda eng, s=s: eng.tensor_copy(
                b1s_s[s][:, 0:32], pb[6][:, 128 + s * 32:160 + s * 32]),
                inc=(s_dve if s == 1 else None))
        tb1 = s_dve.v

        t_attn_dve = s_dve.v
        act_load(SET_GELU)
        slot_pe_done = {}
        slot_dve_done = {}
        for slot in range(2):
            wl_w1 = {}
            wl_w2 = {}
            E.wait("gpsimd", s_dve, tidx)
            if slot == 1:
                E.wait("gpsimd", s_pe, slot_pe_done[0])
            for half in range(2):
                for dc in range(8):
                    E.op("gpsimd", lambda eng, dc=dc, half=half, slot=slot:
                         eng.indirect_dma_start(
                             w1s[:, dc * FF + half * 2048: dc * FF + (half + 1) * 2048],
                             None, w1_d,
                             bass.IndirectOffsetOnAxis(
                                 idxu[:, slot * 8 + dc: slot * 8 + dc + 1], 1),
                             element_offset=half * 2048), inc=s_wl, n=16)
                    wl_w1[half * 8 + dc] = s_wl.v
                rng_fc = range(16) if half == 0 else range(16, 32)
                for fc in rng_fc:
                    E.op("gpsimd", lambda eng, fc=fc, slot=slot:
                         eng.indirect_dma_start(
                             w2s[:, fc * D:(fc + 1) * D], None, w2_d,
                             bass.IndirectOffsetOnAxis(
                                 idxu[:, 16 + slot * 32 + fc: 17 + slot * 32 + fc], 1)),
                         inc=s_wl, n=16)
                    wl_w2[fc] = s_wl.v

            E.wait("scalar", s_dve, tb1)
            x1_tile_free = {}
            ybank_free = {}
            zstore = {}
            zdone = {}
            if slot == 1:
                E.wait("sync", s_gp, mark_slot0)
            nsup = ntg // 2 if ntg >= 2 else 1
            for sup in range(nsup):
                buf = x1t_f[sup % 2]
                if sup == 0:
                    E.wait("sync", s_gp, mark_attn)
                if sup >= 2:
                    E.wait("sync", s_pe, x1_tile_free[sup - 2])
                elif slot == 1:
                    E.wait("sync", s_pe, slot_pe_done[0])
                E.op("sync", lambda eng, buf=buf, sup=sup: eng.dma_start(
                    buf[:].rearrange("p (dc t) -> p dc t", dc=8),
                    x1fm_sc[:, :, sup * 512:(sup + 1) * 512]
                    .rearrange("dc p t -> p dc t")), inc=s_ld, n=16)
                tx1l = s_ld.v
                E.wait("tensor", s_ld, tx1l)
                if sup == 0 and slot == 0:
                    E.wait("tensor", s_dve, t_attn_dve)
                if sup == 0 and slot == 1:
                    E.wait("tensor", s_dve, slot_dve_done[0])
                if sup == 0:
                    E.wait("tensor", s_gp, mark_b2)

                hg_t = {}
                for fc in range(32):
                    hbk = fc % 2
                    if sup == 0:
                        E.wait("tensor", s_wl, wl_w1[(fc // 16) * 8 + 7])
                        E.wait("tensor", s_wl, wl_w2[fc])
                    if fc >= 2:
                        E.wait("tensor", s_act, hg_t[fc - 2])
                    for dc in range(8):
                        E.op("tensor", lambda eng, dc=dc, fc=fc, hbk=hbk, buf=buf:
                             eng.matmul(pb[hbk][:],
                                        w1s[:, dc * FF + fc * 128: dc * FF + (fc + 1) * 128],
                                        buf[:, dc * 512:(dc + 1) * 512],
                                        start=(dc == 0), stop=(dc == 7),
                                        skip_group_check=True),
                             inc=(s_pe if dc == 7 else None))
                    th_ = s_pe.v
                    E.wait("scalar", s_pe, th_)
                    E.op("scalar", lambda eng, fc=fc, hbk=hbk, slot=slot: eng.activation(
                        hcache[:, fc * 512:(fc + 1) * 512], pb[hbk][:], AF.Gelu,
                        bias=b1s_s[slot][:, fc:fc + 1]), inc=s_act)
                    hg_t[fc] = s_act.v
                x1_tile_free[sup] = s_pe.v
                E.wait("tensor", s_act, hg_t[31])
                E.wait("tensor", s_act, hg_t[30])

                for tgh in range(2):
                    tg = sup * 2 + tgh
                    toff = tgh * 256
                    if tg >= 1:
                        E.wait("tensor", s_dve, ybank_free[tg - 1])
                    for th, bks in ((0, (2, 3)), (1, (4, 5))):
                        for halfd in range(2):
                            E.op("tensor", lambda eng, bk=bks[halfd], halfd=halfd, slot=slot:
                                 eng.matmul(pb[bk][:], onesrowb_s[:],
                                            b2rb[slot][:, halfd * 512:(halfd + 1) * 512],
                                            start=True, stop=False, skip_group_check=True),
                                 inc=None)
                    for fc in range(32):
                        for th in range(2):
                            bks = (2, 3) if th == 0 else (4, 5)
                            for halfd in range(2):
                                E.op("tensor", lambda eng, fc=fc, th=th, halfd=halfd,
                                     bks=bks, toff=toff:
                                     eng.matmul(pb[bks[halfd]][:],
                                                hcache[:, fc * 512 + toff + th * 128: fc * 512 + toff + (th + 1) * 128],
                                                w2s[:, fc * D + halfd * 512: fc * D + (halfd + 1) * 512],
                                                start=False, stop=(fc == 31),
                                                skip_group_check=True),
                                     inc=(s_pe if (fc == 31 and th == 1 and halfd == 1) else None))
                    ty = s_pe.v

                    E.wait("vector", s_pe, ty)
                    for th in range(2):
                        tile = tg * 2 + th
                        bks = (2, 3) if th == 0 else (4, 5)
                        if slot == 0:
                            if tg >= 1:
                                E.wait("vector", s_st, zstore.get((tg - 1, th), 0))
                            E.op("vector", lambda eng, th=th, bks=bks: eng.tensor_copy(
                                zc[th][:, 0:512], pb[bks[0]][:]), inc=None)
                            E.op("vector", lambda eng, th=th, bks=bks: eng.tensor_copy(
                                zc[th][:, 512:1024], pb[bks[1]][:]), inc=s_dve)
                            tev = s_dve.v
                            psum_read_t = tev
                            E.wait("gpsimd", s_dve, tev)
                            E.op("gpsimd", lambda eng, th=th, tile=tile: eng.dma_start(
                                y0_sc[tile], zc[th][:]), inc=s_st, n=16)
                            zstore[(tg, th)] = s_st.v
                        else:
                            E.wait("sync", s_dve, zdone.get(tile - 1, 0))
                            E.op("sync", lambda eng, th=th, tile=tile: eng.dma_start(
                                y0c[th][:], y0_sc[tile]), inc=s_ld, n=16)
                            E.op("sync", lambda eng, th=th, tile=tile: eng.dma_start(
                                x1c[th][:], x1tm_sc[tile]), inc=s_ld, n=16)
                            tld2 = s_ld.v
                            E.wait("vector", s_ld, tld2)
                            if tg >= 1:
                                E.wait("vector", s_st, zstore.get((tg - 1, th), 0))
                            E.op("vector", lambda eng, th=th, bks=bks: eng.tensor_scalar(
                                zc[th][:, 0:512], pb[bks[0]][:], rtb[:, 3:4], None, ALU.mult),
                                inc=None)
                            E.op("vector", lambda eng, th=th, bks=bks: eng.tensor_scalar(
                                zc[th][:, 512:1024], pb[bks[1]][:], rtb[:, 3:4], None, ALU.mult),
                                inc=s_dve)
                            psum_read_t = s_dve.v
                            E.wait("vector", s_dve, s_dve.v)
                            E.op("vector", lambda eng, th=th: eng.scalar_tensor_tensor(
                                zc[th][:], y0c[th][:], rtb[:, 2:3], zc[th][:], ALU.mult, ALU.add),
                                inc=s_dve)
                            E.wait("vector", s_dve, s_dve.v)
                            E.op("vector", lambda eng, th=th: eng.tensor_add(
                                zc[th][:], zc[th][:], x1c[th][:]), inc=s_dve)
                            tz = s_dve.v
                            zdone[tile] = tz
                            E.wait("vector", s_dve, tz)
                            E.op("vector", lambda eng, th=th, tile=tile: eng.reduce_sum(
                                statsM[:, tile:tile + 1], zc[th][:], axis=AX.X), inc=s_dve)
                            E.wait("vector", s_dve, s_dve.v)
                            E.op("vector", lambda eng, tile=tile: eng.tensor_scalar(
                                statsM[:, tile:tile + 1], statsM[:, tile:tile + 1],
                                1.0 / D, None, ALU.mult), inc=s_dve)
                            tmean2 = s_dve.v
                            E.wait("scalar", s_dve, tz)
                            E.op("scalar", lambda eng, th=th, tile=tile: eng.activation(
                                sqjunk[:], zc[th][:], AF.Square,
                                accum_out=statsV[:, tile:tile + 1]), inc=s_act)
                            tsq2 = s_act.v
                            E.wait("vector", s_act, tsq2)
                            E.wait("vector", s_dve, tmean2)
                            E.op("vector", lambda eng, tile=tile: eng.tensor_scalar(
                                statsV[:, tile:tile + 1], statsV[:, tile:tile + 1],
                                1.0 / D, None, ALU.mult), inc=s_dve)
                            E.wait("vector", s_dve, s_dve.v)
                            E.op("vector", lambda eng, tile=tile: eng.scalar_tensor_tensor(
                                statsV[:, tile:tile + 1], statsM[:, tile:tile + 1],
                                statsM[:, tile:tile + 1], statsV[:, tile:tile + 1],
                                ALU.mult, ALU.subtract), inc=s_dve)
                            E.wait("gpsimd", s_dve, tz)
                            E.op("gpsimd", lambda eng, th=th, tile=tile: eng.dma_start(
                                z_sc[tile], zc[th][:]), inc=s_st, n=16)
                            zstore[(tg, th)] = s_st.v
                    ybank_free[tg] = psum_read_t
            slot_pe_done[slot] = s_pe.v
            slot_dve_done[slot] = s_dve.v
            if slot == 0:
                E.wait("gpsimd", s_st, s_st.v)
                mark_slot0 = gp_mark()
            else:
                E.wait("gpsimd", s_st, s_st.v)
                mark_z = gp_mark()

        tstats = s_dve.v
        act_load(SET_LNEXP)
        E.wait("scalar", s_dve, tstats)
        E.op("scalar", lambda eng: eng.activation(
            rstdA[:, 0:ntile], statsV[:, 0:ntile], AF.Ln, bias=epsln_s[:, 0:1],
            scale=-1.0), inc=s_act)
        E.wait("scalar", s_act, s_act.v)
        E.op("scalar", lambda eng: eng.activation(
            rstdB[:, 0:ntile], rstdA[:, 0:ntile], AF.Exp, scale=-0.5), inc=s_act)
        trstd2 = s_act.v
        zrel = {}
        out_st = {}
        for tile in range(ntile):
            if tile == 0:
                E.wait("sync", s_gp, mark_z)
            if tile >= 1:
                E.wait("sync", s_dve, zrel[tile - 1])
            E.op("sync", lambda eng, tile=tile: eng.dma_start(
                zin[tile % 2][:], z_sc[tile]), inc=s_ld, n=16)
            tzl = s_ld.v
            E.wait("vector", s_ld, tzl)
            E.wait("vector", s_act, trstd2)
            if tile >= 2:
                E.wait("vector", s_st, out_st[tile - 2])
            E.op("vector", lambda eng, tile=tile: eng.tensor_scalar(
                oout[tile % 2][:], zin[tile % 2][:], statsM[:, tile:tile + 1],
                rstdB[:, tile:tile + 1], ALU.subtract, ALU.mult), inc=s_dve)
            zrel[tile] = s_dve.v
            E.wait("gpsimd", s_dve, zrel[tile])
            E.op("gpsimd", lambda eng, tile=tile, b=b: eng.dma_start(
                out_d[b, tile * C:(tile + 1) * C, :], oout[tile % 2][:]),
                inc=s_st, n=16)
            out_st[tile] = s_st.v

        E.flush()
        fctx.close()

    E.wait("gpsimd", s_st, s_st.v)
    E.wait("sync", s_st, s_st.v)
    E.flush()
    ctx.close()
    return nc


def kernel(L0_attn_x, L0_attn_Wq, L0_attn_Wk, L0_attn_Wv, L0_attn_Wo, L0_attn_proj,
           L0_ffn_Wr, L0_ffn_W1, L0_ffn_b1, L0_ffn_W2, L0_ffn_b2):
    x = np.ascontiguousarray(np.asarray(L0_attn_x, dtype=np.float32))
    nb_total, T, _ = x.shape
    ncores = 8
    nb = nb_total // ncores
    nch = T // C

    nc = build(nb, nch)

    consts = {
        "idf": np.eye(128, dtype=np.float32),
        "idb": np.eye(128).astype(ml_dtypes.bfloat16),
        "tri": np.triu(np.ones((128, 128))).astype(ml_dtypes.bfloat16),
        "onesrow": np.ones((1, 128), np.float32),
        "onescol": np.ones((128, 1), np.float32),
        "onesrowb": np.ones((1, 128)).astype(ml_dtypes.bfloat16),
        "iota": np.arange(128, dtype=np.float32).reshape(128, 1),
        "iota8": np.arange(NEXP, dtype=np.float32).reshape(1, NEXP),
    }
    shared = {
        "wq": np.ascontiguousarray(np.asarray(L0_attn_Wq, np.float32)),
        "wk": np.ascontiguousarray(np.asarray(L0_attn_Wk, np.float32)),
        "wv": np.ascontiguousarray(np.asarray(L0_attn_Wv, np.float32)),
        "wo": np.ascontiguousarray(np.asarray(L0_attn_Wo, np.float32)),
        "proj": np.ascontiguousarray(np.asarray(L0_attn_proj, np.float32)),
        "wr": np.ascontiguousarray(np.asarray(L0_ffn_Wr, np.float32)),
        "w1": np.ascontiguousarray(np.asarray(L0_ffn_W1, np.float32)),
        "b1": np.ascontiguousarray(np.asarray(L0_ffn_b1, np.float32)),
        "w2": np.ascontiguousarray(np.asarray(L0_ffn_W2, np.float32)),
        "b2": np.ascontiguousarray(np.asarray(L0_ffn_b2, np.float32)),
    }
    in_maps = []
    for core in range(ncores):
        m = {"x": x[core * nb:(core + 1) * nb]}
        m.update(consts)
        m.update(shared)
        in_maps.append(m)
    res = run_bass_kernel_spmd(nc, in_maps, core_ids=list(range(ncores)))
    out = np.concatenate([np.asarray(r["out"]) for r in res.results], axis=0)
    return out.astype(np.float32)
